# version 02
# baseline (speedup 1.0000x reference)
import numpy as np
from contextlib import ExitStack

import concourse.bass as bass
import concourse.mybir as mybir
from concourse.bass_utils import run_bass_kernel_spmd

F32 = mybir.dt.float32
BF16 = mybir.dt.bfloat16
I32 = mybir.dt.int32
AF = mybir.ActivationFunctionType
ALU = mybir.AluOpType
AX = mybir.AxisListType

B = 2
S = 8192
D = 1024
DH = 4096
DC = 8
JCN = 32
NBLK = 8
NT = 16
SCALE = 1.0 / 32.0
PEN = -3.2e10

DEBUG = False
import os
PH = int(os.environ.get('BASS_KERNEL_PHASE', '6'))
SUB = int(os.environ.get('BASS_KERNEL_SUB', '9'))


def POS(j, i):
    return (4 * i + (i + j) % 4) * 256


def build_program():
    nc = bass.Bass("TRN2", target_bir_lowering=False, debug=False)

    def din(name, shape, dtype=F32):
        return nc.dram_tensor(name, shape, dtype, kind="ExternalInput").ap()

    x_own = din("x_own", [2048, D])
    wqk = din("wqk", [D, D])
    wv = din("wv", [D, D])
    alibi = din("alibi", [1])
    wexp_own = din("wexp_own", [128, DH])
    wcon_own = din("wcon_own", [DH, 128])
    b_exp_in = din("b_exp", [DH])
    up_bias_in = din("up_bias", [DH])
    b_con_in = din("b_con", [D])
    dn_bias_in = din("dn_bias", [D])
    qpos_in = din("qpos", [2048])
    upvec_sl = din("upvec_sl", [7, 2, 512])
    dnvec_sl = din("dnvec_sl", [7, 2, 512])
    upvec_fm_in = din("upvec_fm", [128, 5 * 2 * JCN])
    dnvec_fm_in = din("dnvec_fm", [128, 5 * 2 * JCN])

    out_own = nc.dram_tensor("out_own", [2048, D], F32, kind="ExternalOutput").ap()
    dbg = {}
    if DEBUG:
        for nm, shp in [("xnfm", [128, DC * 2048]), ("qfm", [128, DC * 2048]),
                        ("extra", [1, 2048]), ("wupc", [128, DH]),
                        ("wdnc", [128, JCN * 128]), ("bmid", [128, JCN]),
                        ("attn", [128, NT * D]), ("xsb", [128, DH]),
                        ("esb", [128, DH])]:
            dbg[nm] = nc.dram_tensor("dbg_" + nm, shp, F32, kind="ExternalOutput").ap()

    ccin_xn = nc.dram_tensor("ccin_xn", [DC * 128, 2048], BF16, kind="Internal").ap()
    xn_all = nc.dram_tensor("xn_all", [4 * DC * 128, 2048], BF16, kind="Internal").ap()
    ccin_v = nc.dram_tensor("ccin_v", [2048, D], BF16, kind="Internal").ap()
    v_all = nc.dram_tensor("v_all", [4 * 2048, D], BF16, kind="Internal").ap()
    ccin_xe = nc.dram_tensor("ccin_xe", [256, 512], BF16, kind="Internal").ap()
    xe_all = nc.dram_tensor("xe_all", [8 * 256, 512], BF16, kind="Internal").ap()
    ccin_wup = nc.dram_tensor("ccin_wup", [128, DH], BF16, kind="Internal").ap()
    wup_all = nc.dram_tensor("wup_all", [8 * 128, DH], BF16, kind="Internal").ap()
    ccin_wdn = nc.dram_tensor("ccin_wdn", [DH, 128], BF16, kind="Internal").ap()
    wdn_all = nc.dram_tensor("wdn_all", [8 * DH, 128], BF16, kind="Internal").ap()

    g4 = [[0, 1, 2, 3], [4, 5, 6, 7]]
    g8 = [list(range(8))]

    main = ExitStack()
    e = main.enter_context
    with main:
        xn_fm = e(nc.sbuf_tensor("xn_fm", [128, DC, 2048], BF16))
        attn_sb = e(nc.sbuf_tensor("attn_sb", [128, NT, D], BF16))
        kpos_f = e(nc.sbuf_tensor("kpos_f", [128, 64], F32))
        alibi_k = e(nc.sbuf_tensor("alibi_k", [128, 64], F32))
        slope_c = e(nc.sbuf_tensor("slope_c", [128, 1], F32))
        ones_f1 = e(nc.sbuf_tensor("ones_f1", [1, 128], F32))
        ones_bc = e(nc.sbuf_tensor("ones_bc", [128, 1], BF16))
        ident_bf = e(nc.sbuf_tensor("ident_bf", [128, 128], BF16))
        ident_f = e(nc.sbuf_tensor("ident_f", [128, 128], F32))
        qpos_bc = e(nc.sbuf_tensor("qpos_bc", [128, NBLK, 256], F32))
        bmid_fm = e(nc.sbuf_tensor("bmid_fm", [128, JCN], F32))
        upv_fm = e(nc.sbuf_tensor("upv_fm", [128, 5 * 2 * JCN], F32))
        dnv_fm = e(nc.sbuf_tensor("dnv_fm", [128, 5 * 2 * JCN], F32))
        iotac_f = e(nc.sbuf_tensor("iotac_f", [128, 1], F32))
        iota4_f = e(nc.sbuf_tensor("iota4_f", [128, 512], F32))

        arena = e(nc.psum_tensor("arena", [128, 4096], F32))

        def A(c0, c1, p0=0, p1=128):
            return arena[p0:p1, c0:c1]
        tpsum = arena[:, 3584:4096].bitcast(BF16)
        tp6 = arena[:, 3072:3584].bitcast(BF16)
        tb0 = arena[:, 0:512].bitcast(BF16)
        tb1 = arena[:, 512:1024].bitcast(BF16)

        sem_names = ("s_ld", "s_x", "s_kv", "s_ws", "s_cc", "s_st", "s_scp",
                     "s_avp", "s_exp", "s_msk", "s_ev", "s_rz", "s_pe", "s_act",
                     "s_dve", "s_pool", "s_hps", "s_gel", "s_yps", "s_ycon",
                     "s_asm", "s_pp_pe", "s_pp_dve", "s_pp_pool", "s_xe", "s_vc", "s_init",
                     "s_xA", "s_xB", "s_kvA", "s_kvB", "s_wsA", "s_wsB",
                     "s_wc", "s_vcA", "s_vcB", "s_stA", "s_stB", "s_xedone")
        SM = {nm: e(nc.semaphore(nm)) for nm in sem_names}
        (s_ld, s_x, s_kv, s_ws, s_cc, s_st, s_scp, s_avp, s_exp, s_msk, s_ev,
         s_rz, s_pe, s_act, s_dve, s_pool, s_hps, s_gel, s_yps, s_ycon, s_asm,
         s_pp_pe, s_pp_dve, s_pp_pool, s_xe, s_vc, s_init, s_xA, s_xB, s_kvA,
         s_kvB, s_wsA, s_wsB, s_wc, s_vcA, s_vcB, s_stA, s_stB, s_xedone) = (
            SM[n] for n in sem_names)
        s_xP = [s_xA, s_xB]
        s_kvP = [s_kvA, s_kvB]
        s_wsP = [s_wsA, s_wsB]
        s_vcP = [s_vcA, s_vcB]
        s_stP = [s_stA, s_stB]

        block = e(nc.Block())

        def sect(ph, engdec):
            if PH >= ph:
                return engdec
            return lambda f: None
        n_ld = [0]

        def colbc(t, coloff, n):
            row = int(np.prod(t.shape[1:]))
            return bass.AP(t, coloff, [[row, 128], [0, n]])


        def pldma(g, out, in_):
            g.dma_start(out=out, in_=in_).then_inc(s_ld, 16)
            n_ld[0] += 16

        sc_qg = ExitStack()
        q_fm = sc_qg.enter_context(nc.sbuf_tensor("q_fm", [128, DC, 2048], BF16))
        extra_row = sc_qg.enter_context(nc.sbuf_tensor("extra_row", [1, 2048], F32))
        wexpT = sc_qg.enter_context(nc.sbuf_tensor("wexpT", [128, JCN, 128], BF16))
        bexp_fm = sc_qg.enter_context(nc.sbuf_tensor("bexp_fm", [128, JCN], BF16))
        upb_fm = sc_qg.enter_context(nc.sbuf_tensor("upb_fm", [128, JCN], F32))

        sc_ln = ExitStack()
        iota_i = sc_ln.enter_context(nc.sbuf_tensor("iota_i", [128, 512], I32))
        x_tile = sc_ln.enter_context(nc.sbuf_tensor("x_tile", [128, 2, D], F32))
        ln_sum = sc_ln.enter_context(nc.sbuf_tensor("ln_sum", [128, NT], F32))
        ln_sq = sc_ln.enter_context(nc.sbuf_tensor("ln_sq", [128, NT], F32))
        rstd = sc_ln.enter_context(nc.sbuf_tensor("rstd", [128, NT], F32))
        nmr = sc_ln.enter_context(nc.sbuf_tensor("nmr", [128, NT], F32))
        var_t = sc_ln.enter_context(nc.sbuf_tensor("var_t", [128, NT], F32))
        std_t = sc_ln.enter_context(nc.sbuf_tensor("std_t", [128, NT], F32))
        m2_t = sc_ln.enter_context(nc.sbuf_tensor("m2_t", [128, NT], F32))
        xn_tm = sc_ln.enter_context(nc.sbuf_tensor("xn_tm", [128, 2, D], BF16))

        sc_wv = ExitStack()
        wv_bf = sc_wv.enter_context(nc.sbuf_tensor("wv_bf", [128, DC, D], BF16))

        sc_vp = ExitStack()
        wvT = sc_vp.enter_context(nc.sbuf_tensor("wvT", [128, DC, D], BF16))
        v_ping = sc_vp.enter_context(nc.sbuf_tensor("v_ping", [128, 2, D], BF16))

        sc_qp = ExitStack()
        wqk_bf = sc_qp.enter_context(nc.sbuf_tensor("wqk_bf", [128, DC, D], BF16))
        q2scr = sc_qp.enter_context(nc.sbuf_tensor("q2scr", [128, 2, 512], BF16))
        rs_c = sc_qp.enter_context(nc.sbuf_tensor("rs_c", [1, 1], F32))

        sc_bx = ExitStack()
        bexp_32 = sc_bx.enter_context(nc.sbuf_tensor("bexp_32", [32, 128], BF16))
        upb_32 = sc_bx.enter_context(nc.sbuf_tensor("upb_32", [32, 128], F32))
        wexp_in = sc_bx.enter_context(nc.sbuf_tensor("wexp_in", [128, DH], BF16))

        @sect(1, block.gpsimd)
        def _(g):
            g.iota(iota_i[:, 128:129], [[0, 1]], channel_multiplier=1)
            g.iota(iota_i[:, 192:256], [[128, 64]], channel_multiplier=1)
            g.drain()
            g.tensor_copy(iotac_f[:, :], iota_i[:, 128:129])
            g.tensor_copy(kpos_f[:, :], iota_i[:, 192:256])
            g.drain()
            g.iota(iota_i[:, 0:512].rearrange("p (a b) -> p a b", b=128),
                   [[0, 4], [1, 128]], channel_multiplier=0)
            g.drain()
            g.tensor_copy(iota4_f[:, :], iota_i[:, 0:512])
            g.memset(ones_f1[:, :], 1.0)
            g.memset(ones_bc[:, :], 1.0)
            g.drain()
            g.engine_nop().then_inc(s_init, 1)
            pldma(g, wqk_bf[:, :, :], wqk.rearrange("(dc p) e -> p dc e", p=128))
            pldma(g, wv_bf[:, :, :], wv.rearrange("(dc p) e -> p dc e", p=128))
            pldma(g, bexp_32[:, :], b_exp_in.rearrange("(a b) -> a b", b=128))
            pldma(g, upb_32[:, :], up_bias_in.rearrange("(a b) -> a b", b=128))
            pldma(g, upv_fm[:, :], upvec_fm_in[:, :])
            pldma(g, dnv_fm[:, :], dnvec_fm_in[:, :])
            pldma(g, slope_c[:, :], bass.AP(alibi.tensor, 0, [[0, 128], [1, 1]]))
            pldma(g, qpos_bc[:, :, :],
                  bass.AP(qpos_in.tensor, 0, [[0, 128], [256, NBLK], [1, 256]]))
            pldma(g, wexp_in[:, :], wexp_own[:, :])
            g.wait_ge(s_ld, n_ld[0])
            g.tensor_tensor(out=alibi_k[:, :], in0=kpos_f[:, :],
                            in1=colbc(slope_c, 0, 64), op=ALU.mult)
            g.drain()
            g.engine_nop().then_inc(s_pool, 1)
        LD_EARLY = n_ld[0]

        @sect(1, block.vector)
        def _(v):
            v.wait_ge(s_init, 1)
            v.tensor_scalar(out=ident_f[:, :], in0=iota4_f[:, 0:128],
                            scalar1=iotac_f[:, 0:1], scalar2=None, op0=ALU.is_equal)
            v.drain()
            v.tensor_copy(ident_bf[:, :], ident_f[:, :])
            v.tensor_scalar(out=iota4_f[:, :], in0=iota4_f[:, :],
                            scalar1=iotac_f[:, 0:1], scalar2=None, op0=ALU.is_equal)
            v.drain()
            v.engine_nop().then_inc(s_init, 1)

        @sect(1, block.sync)
        def _(sy):
            for t in range(NT):
                if t >= 2:
                    sy.wait_ge(s_dve, t - 1)
                    sy.wait_ge(s_act, t - 1)
                sy.dma_start(out=x_tile[:, t % 2, :],
                             in_=x_own[t * 128:(t + 1) * 128, :]).then_inc(s_xP[t % 2], 16)

        @sect(1, block.vector)
        def _(v):
            for t in range(NT):
                v.wait_ge(s_xP[t % 2], 16 * (t // 2 + 1))
                v.reduce_sum(ln_sum[:, t:t + 1], x_tile[:, t % 2, :],
                             axis=AX.X).then_inc(s_dve, 1)
        DVE_L1 = NT

        @sect(1, block.scalar)
        def _(s):
            for t in range(NT):
                s.wait_ge(s_xP[t % 2], 16 * (t // 2 + 1))
                s.activation(out=A(2048, 3072), in_=x_tile[:, t % 2, :], func=AF.Square,
                             accum_out=ln_sq[:, t:t + 1]).then_inc(s_act, 1)
                s.drain()
        ACT_L1 = NT

        @sect(1, block.vector)
        def _(v):
            v.wait_ge(s_dve, DVE_L1)
            v.wait_ge(s_act, ACT_L1)
            v.tensor_scalar(out=ln_sum[:, :], in0=ln_sum[:, :], scalar1=1.0 / D,
                            scalar2=None, op0=ALU.mult)
            v.drain()
            v.tensor_mul(m2_t[:, :], ln_sum[:, :], ln_sum[:, :])
            v.drain()
            v.scalar_tensor_tensor(out=var_t[:, :], in0=ln_sq[:, :], scalar=1.0 / D,
                                   in1=m2_t[:, :], op0=ALU.mult, op1=ALU.subtract)
            v.drain()
            v.tensor_scalar(out=var_t[:, :], in0=var_t[:, :], scalar1=1e-5,
                            scalar2=None, op0=ALU.add).then_inc(s_dve, 1)
        DVE_VAR = DVE_L1 + 1

        @sect(1, block.scalar)
        def _(s):
            s.wait_ge(s_dve, DVE_VAR)
            s.activation(out=std_t[:, :], in_=var_t[:, :], func=AF.Sqrt,
                         bias=0.0, scale=1.0).then_inc(s_act, 1)
        ACT_STD = ACT_L1 + 1

        @sect(1, block.vector)
        def _(v):
            v.wait_ge(s_act, ACT_STD)
            v.reciprocal(rstd[:, :], std_t[:, :])
            v.drain()
            v.tensor_mul(nmr[:, :], ln_sum[:, :], rstd[:, :])
            v.drain()
            v.tensor_scalar(out=nmr[:, :], in0=nmr[:, :], scalar1=-1.0,
                            scalar2=None, op0=ALU.mult).then_inc(s_dve, 1)
        DVE_RSTD = DVE_VAR + 1

        @sect(1, block.sync)
        def _(sy):
            for t in range(NT):
                if t < 2:
                    sy.wait_ge(s_act, ACT_L1 - 1 + t)
                    sy.wait_ge(s_dve, DVE_L1 - 1 + t)
                else:
                    sy.wait_ge(s_act, ACT_STD + t - 1)
                sy.dma_start(out=x_tile[:, t % 2, :],
                             in_=x_own[t * 128:(t + 1) * 128, :]).then_inc(s_xP[t % 2], 16)

        @sect(1, block.scalar)
        def _(s):
            s.wait_ge(s_dve, DVE_RSTD)
            for t in range(NT):
                s.wait_ge(s_xP[t % 2], 16 * (8 + t // 2 + 1))
                if t >= 2:
                    s.wait_ge(s_pe, t - 1)
                s.activation(out=xn_tm[:, t % 2, :], in_=x_tile[:, t % 2, :],
                             func=AF.Identity, bias=nmr[:, t:t + 1],
                             scale=rstd[:, t:t + 1]).then_inc(s_act, 1)
        ACT_XN = ACT_STD + NT

        @sect(1, block.tensor)
        def _(te):
            te.wait_ge(s_pool, 1)
            te.wait_ge(s_init, 2)
            for t in range(NT):
                te.wait_ge(s_act, ACT_STD + t + 1)
                if t >= 1:
                    te.wait_ge(s_dve, DVE_RSTD + t)
                for dc in range(DC):
                    ins = te.transpose(tpsum[:, dc * 128:(dc + 1) * 128],
                                       xn_tm[:, t % 2, dc * 128:(dc + 1) * 128],
                                       ident_bf[:, :])
                ins.then_inc(s_pe, 1)
        PE_XNT = NT

        @sect(1, block.vector)
        def _(v):
            for t in range(NT):
                v.wait_ge(s_pe, t + 1)
                for dc in range(DC):
                    ins = v.tensor_copy(xn_fm[:, dc, t * 128:(t + 1) * 128],
                                        tpsum[:, dc * 128:(dc + 1) * 128])
                ins.then_inc(s_dve, 1)
        DVE_XNF = DVE_RSTD + NT

        @sect(1, block.gpsimd)
        def _(g):
            g.wait_ge(s_dve, DVE_XNF)
            g.dma_start(out=ccin_xn.rearrange("(dc p) t -> p dc t", p=128),
                        in_=xn_fm[:, :, :]).then_inc(s_ld, 16)
            n_ld[0] += 16
            g.wait_ge(s_ld, n_ld[0])
            g.collective_compute("AllGather", ALU.bypass, replica_groups=g4,
                                 ins=[ccin_xn[:, :].opt()],
                                 outs=[xn_all[:, :].opt()]).then_inc(s_cc, 1)

        @sect(2, block.tensor)
        def _(te):
            te.wait_ge(s_ld, LD_EARLY)
            te.wait_ge(s_dve, DVE_XNF)
            n = 0
            for dc in range(DC):
                for ec in range(DC):
                    if n >= 2:
                        te.wait_ge(s_dve, DVE_XNF + n - 1)
                    tp = tpsum if n % 2 == 0 else tp6
                    ins = te.transpose(tp[:, 0:128],
                                       wv_bf[:, ec, dc * 128:(dc + 1) * 128],
                                       ident_bf[:, :])
                    ins.then_inc(s_pe, 1)
                    n += 1
        PE_WVT = PE_XNT + 64

        @sect(2, block.vector)
        def _(v):
            n = 0
            for dc in range(DC):
                for ec in range(DC):
                    v.wait_ge(s_pe, PE_XNT + n + 1)
                    tp = tpsum if n % 2 == 0 else tp6
                    ins = v.tensor_copy(wvT[:, dc, ec * 128:(ec + 1) * 128],
                                        tp[:, 0:128])
                    ins.then_inc(s_dve, 1)
                    n += 1
        DVE_WVT = DVE_XNF + 64

        @sect(2, block.tensor)
        def _(te):
            te.wait_ge(s_dve, DVE_WVT)
            for t in range(NT):
                if t >= 2:
                    te.wait_ge(s_act, ACT_XN + t - 1)
                ps = A(0, 1024) if t % 2 == 0 else A(1024, 2048)
                for dc in range(DC):
                    for h in range(2):
                        ins = te.matmul(ps[:, h * 512:(h + 1) * 512],
                                        xn_fm[:, dc, t * 128:(t + 1) * 128],
                                        wvT[:, dc, h * 512:(h + 1) * 512],
                                        start=(dc == 0), stop=(dc == DC - 1))
                ins.then_inc(s_pe, 1)
        PE_VP = PE_WVT + NT

        @sect(2, block.scalar)
        def _(s):
            for t in range(NT):
                s.wait_ge(s_pe, PE_WVT + t + 1)
                if t >= 2:
                    s.wait_ge(s_vcP[t % 2], 16 * (t // 2))
                ps = A(0, 1024) if t % 2 == 0 else A(1024, 2048)
                s.copy(v_ping[:, t % 2, :], ps[:, :]).then_inc(s_act, 1)
        ACT_VP = ACT_XN + NT

        @sect(2, block.sync)
        def _(sy):
            for t in range(NT):
                sy.wait_ge(s_act, ACT_XN + t + 1)
                sy.dma_start(out=ccin_v[t * 128:(t + 1) * 128, :],
                             in_=v_ping[:, t % 2, :]).then_inc(s_vcP[t % 2], 16)

        @sect(2, block.gpsimd)
        def _(g):
            g.wait_ge(s_vcA, 16 * (NT // 2))
            g.wait_ge(s_vcB, 16 * (NT // 2))
            g.collective_compute("AllGather", ALU.bypass, replica_groups=g4,
                                 ins=[ccin_v[:, :].opt()],
                                 outs=[v_all[:, :].opt()]).then_inc(s_cc, 1)

        @sect(2, block.tensor)
        def _(te):
            te.wait_ge(s_act, ACT_VP)
            n = 0
            for et in range(DC):
                for tb in range(4):
                    if n >= 2:
                        te.wait_ge(s_act, ACT_VP + n - 1)
                    ps = A(0, 512) if n % 2 == 0 else A(512, 1024)
                    for dc in range(DC):
                        ins = te.matmul(ps[:, :],
                                        wqk_bf[:, dc, et * 128:(et + 1) * 128],
                                        xn_fm[:, dc, tb * 512:(tb + 1) * 512],
                                        start=(dc == 0), stop=(dc == DC - 1))
                    ins.then_inc(s_pe, 1)
                    n += 1
        PE_QP = PE_VP + 32

        @sect(2, block.scalar)
        def _(s):
            n = 0
            for et in range(DC):
                for tb in range(4):
                    s.wait_ge(s_pe, PE_VP + n + 1)
                    ps = A(0, 512) if n % 2 == 0 else A(512, 1024)
                    s.copy(q_fm[:, et, tb * 512:(tb + 1) * 512], ps[:, :]).then_inc(s_act, 1)
                    n += 1
        ACT_QP = ACT_VP + 32

        @sect(2, block.scalar)
        def _(s):
            n = 0
            for tb in range(4):
                for et in range(DC):
                    s.wait_ge(s_act, ACT_VP + et * 4 + tb + 1)
                    if n >= 2:
                        s.wait_ge(s_pe, PE_QP + n - 1)
                    s.activation(out=q2scr[:, n % 2, :],
                                 in_=q_fm[:, et, tb * 512:(tb + 1) * 512],
                                 func=AF.Square).then_inc(s_act, 1)
                    n += 1
        ACT_Q2 = ACT_QP + 32

        @sect(2, block.tensor)
        def _(te):
            n = 0
            for tb in range(4):
                for et in range(DC):
                    te.wait_ge(s_act, ACT_QP + n + 1)
                    ins = te.matmul(A(1024 + tb * 512, 1024 + (tb + 1) * 512, 0, 1),
                                    ones_bc[:, :], q2scr[:, n % 2, :],
                                    start=(et == 0), stop=(et == DC - 1))
                    ins.then_inc(s_pe, 1)
                    n += 1
        PE_Q2 = PE_QP + 32

        @sect(2, block.scalar)
        def _(s):
            s.wait_ge(s_pe, PE_Q2)
            s.activation(out=extra_row[:, :], in_=A(1024, 3072, 0, 1), func=AF.Sqrt,
                         bias=0.0, scale=1.0).then_inc(s_act, 1)
            s.activation(out=rs_c[:, :], in_=slope_c[0:1, 0:1], func=AF.Relu,
                         bias=0.0, scale=1.0).then_inc(s_act, 1)
        ACT_BND = ACT_Q2 + 2

        @sect(2, block.vector)
        def _(v):
            v.wait_ge(s_act, ACT_BND)
            v.tensor_scalar(out=extra_row[:, :], in0=extra_row[:, :], scalar1=1.01,
                            scalar2=1.0, op0=ALU.mult, op1=ALU.add)
            v.drain()
            v.scalar_tensor_tensor(out=extra_row[:, :],
                                   in0=qpos_bc[0:1, :, :].rearrange("p a b -> p (a b)"),
                                   scalar=rs_c[0:1, 0:1], in1=extra_row[:, :],
                                   op0=ALU.mult, op1=ALU.add)
            v.drain()
            v.tensor_scalar(out=extra_row[:, :], in0=extra_row[:, :],
                            scalar1=-1.0 / SCALE, scalar2=None,
                            op0=ALU.mult).then_inc(s_dve, 1)
        DVE_EXT = DVE_WVT + 1

        @sect(3 if SUB >= 1 else 99, block.tensor)
        def _(te):
            te.wait_ge(s_ld, LD_EARLY)
            for c in range(JCN):
                if c >= 2:
                    te.wait_ge(s_pp_dve, c - 1)
                tp = tpsum if c % 2 == 0 else tp6
                ins = te.transpose(tp[:, 0:128],
                                   wexp_in[:, c * 128:(c + 1) * 128], ident_bf[:, :])
                ins.then_inc(s_pp_pe, 1)
            te.wait_ge(s_pp_dve, JCN - 1)
            ins = te.transpose(tpsum[:, 0:32], bexp_32[:, :], ident_bf[0:32, 0:32])
            ins.then_inc(s_pp_pe, 1)
            te.wait_ge(s_act, ACT_BND)
            ins = te.transpose(A(2432, 2464), upb_32[:, :], ident_f[0:32, 0:32])
            ins.then_inc(s_pp_pe, 1)
        PP_PE_T0 = JCN + 2

        @sect(3 if SUB >= 1 else 99, block.vector)
        def _(v):
            for c in range(JCN):
                v.wait_ge(s_pp_pe, c + 1)
                tp = tpsum if c % 2 == 0 else tp6
                ins = v.tensor_copy(wexpT[:, c, :], tp[:, 0:128])
                ins.then_inc(s_pp_dve, 1)
            v.wait_ge(s_pp_pe, JCN + 1)
            ins = v.tensor_copy(bexp_fm[:, :], tpsum[:, 0:32])
            ins.then_inc(s_pp_dve, 1)
            v.wait_ge(s_pp_pe, JCN + 2)
            ins = v.tensor_copy(upb_fm[:, :], A(2432, 2464))
            ins.then_inc(s_pp_dve, 1)
        PP_DVE_T0 = JCN + 2

        sc_bx.close()
        sc_qp.close()
        sc_vp.close()
        sc_wv.close()
        sc_ln.close()

        sc_pp = ExitStack()
        app = sc_pp.enter_context
        wcon_in = app(nc.sbuf_tensor("wcon_in", [128, JCN, 128], BF16))
        z_a = app(nc.sbuf_tensor("z_a", [128, JCN, 128], BF16))
        z_t = app(nc.sbuf_tensor("z_t", [128, 16, 128], BF16))
        zd_a = app(nc.sbuf_tensor("zd_a", [128, JCN, 128], BF16))
        b1_fm = app(nc.sbuf_tensor("b1_fm", [128, JCN], F32))
        b2_fm = app(nc.sbuf_tensor("b2_fm", [128, JCN], F32))
        bt_f = app(nc.sbuf_tensor("bt_f", [128, JCN], F32))
        X_sb = app(nc.sbuf_tensor("X_sb", [128, 8, 512], BF16))
        E_sb = app(nc.sbuf_tensor("E_sb", [128, 8, 512], BF16))
        wupc = app(nc.sbuf_tensor("wupc", [128, DH], BF16))
        wdnc = app(nc.sbuf_tensor("wdnc", [128, JCN, 128], BF16))

        sc_xe = ExitStack()
        xe_p = sc_xe.enter_context(nc.sbuf_tensor("xe_p", [128, 2, 512], F32))
        xe_m = sc_xe.enter_context(nc.sbuf_tensor("xe_m", [128, 4, 256], F32))
        xe_w = sc_xe.enter_context(nc.sbuf_tensor("xe_w", [128, 2, 512], F32))
        xe_bf = sc_xe.enter_context(nc.sbuf_tensor("xe_bf", [128, 512], BF16))

        up_seq = [z_a, wexpT, z_a, wexpT, z_a, wexpT]
        dn_seq = [wcon_in, zd_a, wcon_in, zd_a, wcon_in, zd_a]
        Z2_FINAL = up_seq[5]
        ZD_FINAL = dn_seq[5]

        @sect(3, block.gpsimd)
        def _(g):
            g.wait_ge(s_pe, PE_Q2)
            g.wait_ge(s_pp_pe, PP_PE_T0)
            pldma(g, wcon_in[:, :, :], wcon_own.rearrange("(c p) d -> p c d", p=128))
            g.engine_nop().then_inc(s_pool, 1)
        LD_WEXP = n_ld[0]

        def stage_cols(g, vec_sl, sg, transposed, cur):
            sig = 1 << sg
            pldma(g, xe_w[:, 0, :], bass.AP(vec_sl.tensor, (sg * 2 + 0) * 512,
                                            [[0, 128], [1, 512]]))
            pldma(g, xe_w[:, 1, :], bass.AP(vec_sl.tensor, (sg * 2 + 1) * 512,
                                            [[0, 128], [1, 512]]))
            g.wait_ge(s_ld, n_ld[0])
            nb = 512 // (2 * sig)

            def vw(apobj, off):
                return bass.AP(apobj.tensor, apobj.offset + off,
                               [list(apobj.ap[0]), [2 * sig, nb], [1, sig]])

            def mw(k):
                return bass.AP(xe_m, k * 256, [[1024, 128], [sig, nb], [1, sig]])
            src = xe_p[:, cur, :]
            dst = xe_p[:, 1 - cur, :]
            a_in, b_in = vw(src, 0), vw(src, sig)
            dg_a, dg_b = vw(xe_w[:, 0, :], 0), vw(xe_w[:, 0, :], sig)
            of_a, of_b = vw(xe_w[:, 1, :], 0), vw(xe_w[:, 1, :], sig)
            na, nb_ = vw(dst, 0), vw(dst, sig)
            g.tensor_mul(mw(0), a_in, dg_a)
            g.tensor_mul(mw(1), b_in, of_b if transposed else of_a)
            g.tensor_mul(mw(2), b_in, dg_b)
            g.tensor_mul(mw(3), a_in, of_a if transposed else of_b)
            g.drain()
            g.tensor_add(na, mw(0), mw(1))
            g.tensor_add(nb_, mw(2), mw(3))
            g.drain()

        @sect(3 if SUB >= 2 else 99, block.gpsimd)
        def _(g):
            g.wait_ge(s_pool, 1)
            g.wait_ge(s_init, 2)
            g.wait_ge(s_pe, PE_Q2)
            g.wait_ge(s_pp_pe, PP_PE_T0)
            g.wait_ge(s_pp_dve, PP_DVE_T0)
            for half, (vec_sl, stages, tr) in enumerate(
                    [(upvec_sl, list(range(7)), False),
                     (dnvec_sl, list(range(6, -1, -1)), True)]):
                g.tensor_copy(xe_p[:, 0, :], iota4_f[:, :])
                g.drain()
                cur = 0
                for sg in stages:
                    stage_cols(g, vec_sl, sg, tr, cur)
                    cur = 1 - cur
                g.tensor_copy(xe_bf[:, :], xe_p[:, cur, :])
                g.drain()
                g.dma_start(out=ccin_xe[half * 128:(half + 1) * 128, :],
                            in_=xe_bf[:, :]).then_inc(s_ld, 16)
                n_ld[0] += 16
            g.wait_ge(s_ld, n_ld[0])
            g.engine_nop().then_inc(s_xedone, 1)
            g.collective_compute("AllGather", ALU.bypass, replica_groups=g8,
                                 ins=[ccin_xe[:, :].opt()],
                                 outs=[xe_all[:, :].opt()]).then_inc(s_cc, 1)
            g.wait_ge(s_cc, 3)
            g.wait_ge(s_pp_pe, JCN + 2)
            g.dma_start(out=X_sb[:, :, :],
                        in_=xe_all.rearrange("(r h p) c -> h p r c", r=8, p=128)[0]
                        ).then_inc(s_xe, 16)
            g.dma_start(out=E_sb[:, :, :],
                        in_=xe_all.rearrange("(r h p) c -> h p r c", r=8, p=128)[1]
                        ).then_inc(s_xe, 16)

        def hi_stage(g, s_idx, src, dst, vecs_fm, transposed):
            sc = 1 << (s_idx - 7)

            def cvec(t, c):
                return colbc(vecs_fm, ((s_idx - 7) * 2 + t) * JCN + c, 128)
            pairs = [(c, c ^ sc) for c in range(JCN) if not (c & sc)]
            for q in range(4):
                sub = pairs[q * 4:(q + 1) * 4]
                for idx, (ca, cb) in enumerate(sub):
                    o_a = cvec(1, cb) if transposed else cvec(1, ca)
                    o_b = cvec(1, ca) if transposed else cvec(1, cb)
                    g.tensor_mul(z_t[:, 4 * idx + 0, :], src[:, ca, :], cvec(0, ca))
                    g.tensor_mul(z_t[:, 4 * idx + 1, :], src[:, cb, :], o_a)
                    g.tensor_mul(z_t[:, 4 * idx + 2, :], src[:, cb, :], cvec(0, cb))
                    g.tensor_mul(z_t[:, 4 * idx + 3, :], src[:, ca, :], o_b)
                g.drain()
                for idx, (ca, cb) in enumerate(sub):
                    g.tensor_add(dst[:, ca, :], z_t[:, 4 * idx + 0, :],
                                 z_t[:, 4 * idx + 1, :])
                    g.tensor_add(dst[:, cb, :], z_t[:, 4 * idx + 2, :],
                                 z_t[:, 4 * idx + 3, :])
                g.drain()

        @sect(3 if SUB >= 3 else 99, block.gpsimd)
        def _(g):
            for k, s_idx in enumerate([11, 10, 9, 8, 7]):
                hi_stage(g, s_idx, dn_seq[k], dn_seq[k + 1], dnv_fm, transposed=True)
            g.engine_nop().then_inc(s_pp_pool, 1)

        sc_xe.close()
        sc_att = ExitStack()
        kv_k = sc_att.enter_context(nc.sbuf_tensor("kv_k", [128, 2, DC, 256], BF16))
        kv_v = sc_att.enter_context(nc.sbuf_tensor("kv_v", [128, 2, 2, D], BF16))
        pt_sb = sc_att.enter_context(nc.sbuf_tensor("pt_sb", [128, 2, 256], BF16))
        pen_sb = sc_att.enter_context(nc.sbuf_tensor("pen_sb", [128, 256], BF16))
        rz_sb = sc_att.enter_context(nc.sbuf_tensor("rz_sb", [128, 2], F32))

        seg_end_kt = []
        _g = 0
        for i in range(NBLK):
            for sg in range(4 * (i + 1)):
                seg_end_kt.append(_g + (sg + 1) * 2)
            _g += 8 * (i + 1)
        TOT_KT = _g

        def av_slice(sub):
            return A(0, 1024) if sub == 0 else A(1024, 2048)

        @sect(4, block.sync)
        def _(sy):
            sy.wait_ge(s_cc, 2)
            sy.wait_ge(s_pe, PE_Q2)
            sy.wait_ge(s_pp_pe, PP_PE_T0)
            sy.wait_ge(s_pp_dve, PP_DVE_T0)
            sy.wait_ge(s_xedone, 1)
            g_seg = 0
            for i in range(NBLK):
                for sg in range(4 * (i + 1)):
                    if g_seg >= 2:
                        sy.wait_ge(s_avp, seg_end_kt[g_seg - 2])
                    par = g_seg % 2
                    i_star, sig = sg // 4, sg % 4
                    r, o = (sig - i_star) % 4, i_star * 256
                    sy.dma_start(
                        out=kv_k[:, par, :, :],
                        in_=xn_all.rearrange("(r dc p) t -> r p dc t", r=4, dc=8)
                        [r, :, :, o:o + 256]).then_inc(s_kvP[par], 16)
                    sy.dma_start(
                        out=kv_v[:, par, :, :],
                        in_=v_all.rearrange("(r kt p) e -> r p kt e", r=4, p=128)
                        [r, :, o // 128:o // 128 + 2, :]).then_inc(s_kvP[par], 16)
                    g_seg += 1

        @sect(4, block.tensor)
        def _(te):
            te.wait_ge(s_dve, DVE_EXT)
            te.wait_ge(s_act, ACT_QP)
            n_kt = 0
            g_seg = 0
            for i in range(NBLK):
                if i >= 1:
                    te.wait_ge(s_ev, i)
                if i == 1:
                    te.wait_ge(s_pp_dve, PP_DVE_T0 + JCN + 1)
                if i == 3:
                    te.wait_ge(s_pp_dve, PP_DVE_T0 + 2 * JCN + 1)
                if i == 5:
                    te.wait_ge(s_pp_dve, PP_DVE_T0 + 3 * JCN + 1)
                nkt_blk = 8 * (i + 1)

                def emit_av(kt_loc, n_av):
                    te.wait_ge(s_exp, n_av + 1)
                    par_av = (g_seg0 + kt_loc // 8) % 2
                    kk_av = (kt_loc // 2) % 4 * 0 + (kt_loc % 2)
                    sgl = kt_loc // 2
                    par_av = (g_seg0 + sgl) % 2
                    first = (kt_loc == 0)
                    last = (kt_loc == nkt_blk - 1)
                    for sub in range(2):
                        lt = pt_sb[:, n_av % 2, sub * 128:(sub + 1) * 128]
                        ins = te.matmul(av_slice(sub)[:, 0:512], lt,
                                        kv_v[:, par_av, kk_av, 0:512],
                                        start=first, stop=last)
                        ins = te.matmul(av_slice(sub)[:, 512:1024], lt,
                                        kv_v[:, par_av, kk_av, 512:1024],
                                        start=first, stop=last)
                        ins = te.matmul(A(2048 + 512 * sub, 2049 + 512 * sub), lt,
                                        ones_bc[:, :], start=first, stop=last)
                    ins.then_inc(s_avp, 1)

                g_seg0 = g_seg
                for sg in range(4 * (i + 1)):
                    par = g_seg % 2
                    te.wait_ge(s_kvP[par], 32 * (g_seg // 2 + 1))
                    for kk in range(2):
                        ktg = sg * 2 + kk
                        sc = A(3072, 3328) if n_kt % 2 == 0 else A(3584, 3840)
                        if n_kt >= 2:
                            te.wait_ge(s_exp, n_kt - 1)
                        for dc in range(DC):
                            ins = te.matmul(sc[:, :],
                                            kv_k[:, par, dc, kk * 128:(kk + 1) * 128],
                                            q_fm[:, dc, i * 256:(i + 1) * 256],
                                            start=(dc == 0), stop=False)
                        ins = te.matmul(sc[:, :], ones_f1[:, :],
                                        extra_row[:, i * 256:(i + 1) * 256],
                                        start=False, stop=True)
                        ins.then_inc(s_scp, 1)
                        if ktg >= 1:
                            emit_av(ktg - 1, n_kt - 1)
                        n_kt += 1
                    g_seg += 1
                emit_av(nkt_blk - 1, n_kt - 1)
                if i == 0:
                    te.wait_ge(s_ev, i + 1)
                    te.wait_ge(s_xe, 32)
                    te.wait_ge(s_pp_dve, PP_DVE_T0)
                    for c in range(JCN):
                        if c >= 2:
                            te.wait_ge(s_pp_dve, PP_DVE_T0 + c - 1)
                        Xc = X_sb[:, c // 4, (c % 4) * 128:(c % 4) * 128 + 128]
                        ins = te.matmul(A(2432, 2560) if c % 2 == 0 else A(2688, 2816),
                                        Xc, wexpT[:, c, :], start=True, stop=True)
                        ins.then_inc(s_pp_pe, 1)
                    te.wait_ge(s_pp_dve, PP_DVE_T0 + JCN)
                    for c in range(JCN):
                        Xc = X_sb[:, c // 4, (c % 4) * 128:(c % 4) * 128 + 128]
                        ins = te.matmul(A(2576 + c, 2577 + c), Xc,
                                        bexp_fm[:, c:c + 1], start=True, stop=True)
                    ins.then_inc(s_pp_pe, 1)
                if i == 2:
                    te.wait_ge(s_ev, i + 1)
                    te.wait_ge(s_pp_pool, 1)
                    te.wait_ge(s_xe, 32)
                    te.wait_ge(s_pp_dve, PP_DVE_T0 + JCN + 1)
                    for c in range(JCN):
                        if c >= 2:
                            te.wait_ge(s_pp_dve, PP_DVE_T0 + JCN + 1 + c - 1)
                        Ec = E_sb[:, c // 4, (c % 4) * 128:(c % 4) * 128 + 128]
                        ins = te.matmul(A(2432, 2560) if c % 2 == 0 else A(2688, 2816),
                                        Ec, ZD_FINAL[:, c, :], start=True, stop=True)
                        ins.then_inc(s_pp_pe, 1)
                if i == 4:
                    te.wait_ge(s_ev, i + 1)
                    te.wait_ge(s_pp_pool, 2)
                    for c in range(JCN):
                        if c >= 2:
                            te.wait_ge(s_pp_dve, PP_DVE_T0 + 2 * JCN + 1 + c - 1)
                        tb = tb0 if c % 2 == 0 else tb1
                        ins = te.transpose(tb[:, 0:128],
                                           Z2_FINAL[:, c, :], ident_bf[:, :])
                        ins.then_inc(s_pp_pe, 1)

        @sect(4, block.vector)
        def _(v):
            v.wait_ge(s_ld, LD_EARLY)
            n_kt = 0
            n_msk = 0
            for i in range(NBLK):
                for sg in range(4 * (i + 1)):
                    for kk in range(2):
                        ktg = sg * 2 + kk
                        if ktg >= 8 * i:
                            sc = A(3072, 3328) if n_kt % 2 == 0 else A(3584, 3840)
                            v.wait_ge(s_scp, n_kt + 1)
                            v.tensor_scalar(out=pen_sb[:, :], in0=qpos_bc[:, i, :],
                                            scalar1=kpos_f[:, ktg:ktg + 1],
                                            scalar2=PEN, op0=ALU.is_lt, op1=ALU.mult)
                            v.drain()
                            ins = v.tensor_add(sc[:, :], sc[:, :], pen_sb[:, :])
                            ins.then_inc(s_msk, 1)
                            v.drain()
                            n_msk += 1
                        n_kt += 1
                v.wait_ge(s_avp, n_kt)
                v.reciprocal(rz_sb[:, :], bass.AP(arena, 2048, [[4096, 128], [512, 2]]))
                v.drain()
                v.engine_nop().then_inc(s_rz, 1)
                if i == 0:
                    for c in range(JCN):
                        v.wait_ge(s_pp_pe, PP_PE_T0 + c + 1)
                        ps = A(2432, 2560) if c % 2 == 0 else A(2688, 2816)
                        ins = v.tensor_copy(z_a[:, c, :], ps[:, :])
                        ins.then_inc(s_pp_dve, 1)
                    v.wait_ge(s_pp_pe, PP_PE_T0 + JCN + 1)
                    ins = v.tensor_copy(b1_fm[:, :], A(2576, 2608))
                    ins.then_inc(s_pp_dve, 1)
                if i == 2:
                    for c in range(JCN):
                        v.wait_ge(s_pp_pe, PP_PE_T0 + JCN + 1 + c + 1)
                        ps = A(2432, 2560) if c % 2 == 0 else A(2688, 2816)
                        ins = v.tensor_copy(wdnc[:, c, :], ps[:, :])
                        ins.then_inc(s_pp_dve, 1)
                if i == 4:
                    for c in range(JCN):
                        v.wait_ge(s_pp_pe, PP_PE_T0 + 2 * JCN + 1 + c + 1)
                        tb = tb0 if c % 2 == 0 else tb1
                        ins = v.tensor_copy(wupc[:, c * 128:(c + 1) * 128],
                                            tb[:, 0:128])
                        ins.then_inc(s_pp_dve, 1)

        @sect(4, block.scalar)
        def _(s):
            s.wait_ge(s_pool, 1)
            n_kt = 0
            n_msk = 0
            for i in range(NBLK):
                for sg in range(4 * (i + 1)):
                    for kk in range(2):
                        ktg = sg * 2 + kk
                        sc = A(3072, 3328) if n_kt % 2 == 0 else A(3584, 3840)
                        s.wait_ge(s_scp, n_kt + 1)
                        if ktg >= 8 * i:
                            n_msk += 1
                            s.wait_ge(s_msk, n_msk)
                        if n_kt >= 2:
                            s.wait_ge(s_avp, n_kt - 1)
                        ins = s.activation(out=pt_sb[:, n_kt % 2, :], in_=sc[:, :],
                                           func=AF.Exp, bias=alibi_k[:, ktg:ktg + 1],
                                           scale=SCALE)
                        ins.then_inc(s_exp, 1)
                        n_kt += 1
                s.wait_ge(s_rz, i + 1)
                for sub in range(2):
                    ins = s.activation(out=attn_sb[:, 2 * i + sub, :],
                                       in_=av_slice(sub), func=AF.Identity,
                                       bias=0.0, scale=rz_sb[:, sub:sub + 1])
                ins.then_inc(s_ev, 1)

        @sect(4, block.gpsimd)
        def _(g):
            g.wait_ge(s_pp_dve, PP_DVE_T0 + JCN + 1)
            for k, s_idx in enumerate([7, 8, 9, 10, 11]):
                hi_stage(g, s_idx, up_seq[k], up_seq[k + 1], upv_fm, transposed=False)
            bb = [b1_fm, b2_fm]
            bcur = 0
            for s_idx in [7, 8, 9, 10, 11]:
                sc_ = 1 << (s_idx - 7)

                def col(t, c):
                    base = ((s_idx - 7) * 2 + t) * JCN + c
                    return upv_fm[:, base:base + 1]
                pairs = [(c, c ^ sc_) for c in range(JCN) if not (c & sc_)]
                for ca, cb in pairs:
                    g.tensor_mul(bt_f[:, ca:ca + 1], bb[bcur][:, cb:cb + 1], col(1, ca))
                    g.tensor_mul(bt_f[:, cb:cb + 1], bb[bcur][:, ca:ca + 1], col(1, cb))
                g.drain()
                for ca, cb in pairs:
                    g.tensor_mul(bb[1 - bcur][:, ca:ca + 1], bb[bcur][:, ca:ca + 1],
                                 col(0, ca))
                    g.tensor_mul(bb[1 - bcur][:, cb:cb + 1], bb[bcur][:, cb:cb + 1],
                                 col(0, cb))
                g.drain()
                for ca, cb in pairs:
                    g.tensor_add(bb[1 - bcur][:, ca:ca + 1], bb[1 - bcur][:, ca:ca + 1],
                                 bt_f[:, ca:ca + 1])
                    g.tensor_add(bb[1 - bcur][:, cb:cb + 1], bb[1 - bcur][:, cb:cb + 1],
                                 bt_f[:, cb:cb + 1])
                g.drain()
                bcur = 1 - bcur
            g.wait_ge(s_pp_dve, PP_DVE_T0)
            g.tensor_add(bmid_fm[:, :], bb[bcur][:, :], upb_fm[:, :])
            g.drain()
            g.engine_nop().then_inc(s_pp_pool, 1)
            g.engine_nop().then_inc(s_pool, 1)
            g.wait_ge(s_pp_dve, PP_DVE_T0 + JCN + 1 + JCN)
            g.dma_start(out=ccin_wdn.rearrange("(c p) d -> p c d", p=128),
                        in_=wdnc[:, :, :]).then_inc(s_ld, 16)
            n_ld[0] += 16
            g.wait_ge(s_ld, n_ld[0])
            g.collective_compute("AllGather", ALU.bypass, replica_groups=g8,
                                 ins=[ccin_wdn[:, :].opt()],
                                 outs=[wdn_all[:, :].opt()]).then_inc(s_cc, 1)
            g.wait_ge(s_pp_dve, PP_DVE_T0 + 2 * JCN + 1 + JCN)
            g.dma_start(out=ccin_wup[:, :], in_=wupc[:, :]).then_inc(s_ld, 16)
            n_ld[0] += 16
            g.wait_ge(s_ld, n_ld[0])
            g.collective_compute("AllGather", ALU.bypass, replica_groups=g8,
                                 ins=[ccin_wup[:, :].opt()],
                                 outs=[wup_all[:, :].opt()]).then_inc(s_cc, 1)

        if DEBUG:
            @block.gpsimd
            def _(g):
                def dump(o, i_):
                    g.dma_start(out=o, in_=i_).then_inc(s_ld, 16)
                    n_ld[0] += 16
                g.wait_ge(s_dve, DVE_XNF)
                dump(dbg["xnfm"][:, :], xn_fm.rearrange("p dc t -> p (dc t)"))
                g.wait_ge(s_act, ACT_QP)
                dump(dbg["qfm"][:, :], q_fm.rearrange("p dc t -> p (dc t)"))
                g.wait_ge(s_dve, DVE_EXT)
                dump(dbg["extra"][:, :], extra_row[:, :])
                g.wait_ge(s_xe, 32)
                dump(dbg["xsb"][:, :], X_sb.rearrange("p r c -> p (r c)"))
                dump(dbg["esb"][:, :], E_sb.rearrange("p r c -> p (r c)"))
                g.wait_ge(s_pp_dve, PP_DVE_T0 + JCN + 1 + JCN)
                dump(dbg["wdnc"][:, :], wdnc.rearrange("p c d -> p (c d)"))
                g.wait_ge(s_pp_dve, PP_DVE_T0 + 2 * JCN + 1 + JCN)
                dump(dbg["wupc"][:, :], wupc[:, :])
                g.wait_ge(s_pool, 3)
                dump(dbg["bmid"][:, :], bmid_fm[:, :])
                g.wait_ge(s_ev, NBLK)
                dump(dbg["attn"][:, :], attn_sb.rearrange("p t e -> p (t e)"))

        if not DEBUG:
            sc_att.close()
            sc_pp.close()
            sc_qg.close()
        if PH >= 6:
            hg = e(nc.sbuf_tensor("hg", [128, 16, 1024], BF16))
            bcon_row = e(nc.sbuf_tensor("bcon_row", [1, D], F32))
            dnb_row = e(nc.sbuf_tensor("dnb_row", [1, D], F32))
            y1 = e(nc.sbuf_tensor("y1", [128, 8, D], BF16))
            wup_buf = e(nc.sbuf_tensor("wup_buf", [128, 2, DC, 128], BF16))
            wcon_buf = e(nc.sbuf_tensor("wcon_buf", [128, 16, D], BF16))
            x_re = e(nc.sbuf_tensor("x_re", [128, 2, D], F32))
            outb = e(nc.sbuf_tensor("outb", [128, 2, D], F32))
            t1_sb = e(nc.sbuf_tensor("t1_sb", [128, 2, D], F32))

        if PH < 6:
            @block.gpsimd
            def _(g):
                if PH >= 1:
                    g.wait_ge(s_cc, 1)
                    g.wait_ge(s_dve, DVE_XNF)
                if PH >= 2:
                    g.wait_ge(s_cc, 2)
                    g.wait_ge(s_dve, DVE_EXT)
                    g.wait_ge(s_act, ACT_BND)
                if PH >= 3:
                    if SUB >= 1:
                        g.wait_ge(s_pp_dve, PP_DVE_T0)
                    if SUB >= 2:
                        g.wait_ge(s_xe, 32)
                    if SUB >= 3:
                        g.wait_ge(s_pp_pool, 1)
                if PH >= 4:
                    g.wait_ge(s_ev, NBLK)
                if PH >= 5:
                    g.wait_ge(s_cc, 5)
                    g.wait_ge(s_pool, 3)
                for t in range(NT):
                    g.dma_start(out=out_own[t * 128:(t + 1) * 128, :],
                                in_=attn_sb[:, t, :]).then_inc(s_st, 16)
                g.wait_ge(s_st, 16 * 16)

        @sect(6, block.sync)
        def _(sy):
            sy.wait_ge(s_cc, 5)
            sy.wait_ge(s_ev, NBLK)
            sy.dma_start(out=bcon_row[:, :], in_=b_con_in[None, :]).then_inc(s_wc, 16)
            sy.dma_start(out=dnb_row[:, :], in_=dn_bias_in[None, :]).then_inc(s_wc, 16)
            nxr = 0
            for th in range(2):
                for hf in range(2):
                    u = th * 2 + hf
                    for jtl in range(16):
                        jt = hf * 16 + jtl
                        gjt = u * 16 + jtl
                        if gjt >= 2:
                            sy.wait_ge(s_hps, gjt - 1)
                        sy.dma_start(
                            out=wup_buf[:, gjt % 2, :, :],
                            in_=wup_all.rearrange("(dc p) (jt c) -> jt p dc c",
                                                  p=128, c=128)[jt]).then_inc(s_wsP[gjt % 2], 16)
                    if u >= 1:
                        sy.wait_ge(s_yps, u * 8)
                    for jcl in range(16):
                        jc = hf * 16 + jcl
                        sy.dma_start(
                            out=wcon_buf[:, jcl, :].rearrange("p (r d) -> p r d", d=128),
                            in_=wdn_all.rearrange("(r c p) d -> c p r d",
                                                  r=8, p=128)[jc]).then_inc(s_wc, 16)
                    if hf == 1:
                        for tl in range(10):
                            if tl < 8:
                                tt = th * 8 + tl
                                if nxr >= 2:
                                    sy.wait_ge(s_asm, nxr - 1)
                                sy.dma_start(out=x_re[:, tt % 2, :],
                                             in_=x_own[tt * 128:(tt + 1) * 128, :]
                                             ).then_inc(s_xP[tt % 2], 16)
                                nxr += 1
                            if tl >= 2:
                                ts_ = th * 8 + tl - 2
                                sy.wait_ge(s_asm, ts_ + 1)
                                sy.dma_start(out=out_own[ts_ * 128:(ts_ + 1) * 128, :],
                                             in_=outb[:, ts_ % 2, :]).then_inc(s_stP[ts_ % 2], 16)
            sy.wait_ge(s_stA, 16 * 8)
            sy.wait_ge(s_stB, 16 * 8)

        @sect(6, block.tensor)
        def _(te):
            te.wait_ge(s_ev, NBLK)
            for th in range(2):
                for hf in range(2):
                    u = th * 2 + hf
                    for jtl in range(16):
                        gjt = u * 16 + jtl
                        gw = u * 16 + jtl
                        te.wait_ge(s_wsP[gw % 2], 16 * (gw // 2 + 1))
                        if gjt >= 2:
                            te.wait_ge(s_gel, gjt - 1)
                        hp = A(0, 1024) if gjt % 2 == 0 else A(1024, 2048)
                        for dc in range(DC):
                            for hh in range(2):
                                ins = te.matmul(
                                    hp[:, hh * 512:(hh + 1) * 512],
                                    wup_buf[:, gjt % 2, dc, :],
                                    xn_fm[:, dc, th * 1024 + hh * 512:
                                          th * 1024 + (hh + 1) * 512],
                                    start=(dc == 0), stop=(dc == DC - 1))
                        ins.then_inc(s_hps, 1)
                    te.wait_ge(s_wc, 32 + 16 * (u + 1) * 16)
                    te.wait_ge(s_gel, u * 16 + 16)
                    for tl in range(8):
                        gt = u * 8 + tl
                        if gt >= 2:
                            te.wait_ge(s_ycon, gt - 1)
                        yp = A(2048, 3072) if gt % 2 == 0 else A(3072, 4096)
                        for jcl in range(16):
                            for hh in range(2):
                                ins = te.matmul(
                                    yp[:, hh * 512:(hh + 1) * 512],
                                    hg[:, jcl, tl * 128:(tl + 1) * 128],
                                    wcon_buf[:, jcl, hh * 512:(hh + 1) * 512],
                                    start=(jcl == 0),
                                    stop=(hf == 0 and jcl == 15))
                        if hf == 1:
                            for hh in range(2):
                                ins = te.matmul(yp[:, hh * 512:(hh + 1) * 512],
                                                ones_f1[:, :],
                                                bcon_row[:, hh * 512:(hh + 1) * 512],
                                                start=False, stop=False)
                                ins = te.matmul(yp[:, hh * 512:(hh + 1) * 512],
                                                ones_f1[:, :],
                                                dnb_row[:, hh * 512:(hh + 1) * 512],
                                                start=False, stop=True)
                        ins.then_inc(s_yps, 1)

        @sect(6, block.scalar)
        def _(s):
            s.wait_ge(s_pool, 3)
            for th in range(2):
                for hf in range(2):
                    u = th * 2 + hf
                    if u >= 1:
                        s.wait_ge(s_yps, u * 8)
                    for jtl in range(16):
                        jt = hf * 16 + jtl
                        gjt = u * 16 + jtl
                        s.wait_ge(s_hps, gjt + 1)
                        hp = A(0, 1024) if gjt % 2 == 0 else A(1024, 2048)
                        ins = s.activation(out=hg[:, jtl, :], in_=hp[:, :],
                                           func=AF.Gelu, bias=bmid_fm[:, jt:jt + 1],
                                           scale=1.0)
                        ins.then_inc(s_gel, 1)
                    if hf == 0:
                        for tl in range(8):
                            gt = u * 8 + tl
                            s.wait_ge(s_yps, gt + 1)
                            yp = A(2048, 3072) if gt % 2 == 0 else A(3072, 4096)
                            ins = s.copy(y1[:, tl, :], yp[:, :])
                            ins.then_inc(s_ycon, 1)

        @sect(6, block.vector)
        def _(v):
            for th in range(2):
                u = th * 2 + 1
                for tl in range(8):
                    tt = th * 8 + tl
                    gt = u * 8 + tl
                    v.wait_ge(s_yps, gt + 1)
                    v.wait_ge(s_xP[tt % 2], 16 * (16 + tt // 2 + 1))
                    if tt >= 2:
                        v.wait_ge(s_stP[tt % 2], 16 * (tt // 2))
                    yp = A(2048, 3072) if gt % 2 == 0 else A(3072, 4096)
                    ins = v.scalar_tensor_tensor(out=t1_sb[:, tt % 2, :], in0=yp[:, :],
                                                 scalar=1.0, in1=y1[:, tl, :],
                                                 op0=ALU.mult, op1=ALU.add)
                    ins.then_inc(s_ycon, 1)
                    v.drain()
                    v.tensor_add(t1_sb[:, tt % 2, :], t1_sb[:, tt % 2, :],
                                 attn_sb[:, tt, :])
                    v.drain()
                    ins = v.tensor_add(outb[:, tt % 2, :], t1_sb[:, tt % 2, :],
                                       x_re[:, tt % 2, :])
                    ins.then_inc(s_asm, 1)
                    v.drain()
        if DEBUG:
            sc_att.close()
            sc_pp.close()
            sc_qg.close()
    return nc


_NC_CACHE = {}


def _get_nc():
    if "nc" not in _NC_CACHE:
        _NC_CACHE["nc"] = build_program()
    return _NC_CACHE["nc"]


def _stage_vectors(weights, dim):
    ns = weights.shape[0]
    out = np.zeros((ns, 2, dim), np.float32)
    j = np.arange(dim)
    for s in range(ns):
        sig = 1 << s
        w = np.asarray(weights[s], np.float32)
        pair = (j >> (s + 1)) * sig + (j & (sig - 1))
        bit = (j & sig) != 0
        out[s, 0] = np.where(bit, w[pair, 1, 1], w[pair, 0, 0])
        out[s, 1] = np.where(bit, w[pair, 1, 0], w[pair, 0, 1])
    return out


def kernel(L0_attn_x, L0_attn_W_qk, L0_attn_W_v, L0_attn_alibi_slope,
           L0_ffn_W_exp, L0_ffn_b_exp, L0_ffn_up_weights, L0_ffn_up_bias,
           L0_ffn_down_weights, L0_ffn_W_con, L0_ffn_b_con, L0_ffn_down_bias):
    x = np.ascontiguousarray(np.asarray(L0_attn_x, np.float32))
    upv = _stage_vectors(np.asarray(L0_ffn_up_weights), DH)
    dnv = _stage_vectors(np.asarray(L0_ffn_down_weights), DH)

    def fm_slices(vec):
        out = np.zeros((128, 5 * 2 * JCN), np.float32)
        for s in range(7, 12):
            for t in range(2):
                out[:, ((s - 7) * 2 + t) * JCN:((s - 7) * 2 + t + 1) * JCN] = \
                    vec[s, t].reshape(JCN, 128).T
        return np.ascontiguousarray(out)

    upvec_fm = fm_slices(upv)
    dnvec_fm = fm_slices(dnv)
    wqk_c = np.ascontiguousarray(np.asarray(L0_attn_W_qk, np.float32))
    wv_c = np.ascontiguousarray(np.asarray(L0_attn_W_v, np.float32))
    al_c = np.ascontiguousarray(np.asarray(L0_attn_alibi_slope, np.float32))
    wexp_c = np.asarray(L0_ffn_W_exp, np.float32)
    wcon_c = np.asarray(L0_ffn_W_con, np.float32)
    bexp_c = np.ascontiguousarray(np.asarray(L0_ffn_b_exp, np.float32))
    upb_c = np.ascontiguousarray(np.asarray(L0_ffn_up_bias, np.float32))
    bcon_c = np.ascontiguousarray(np.asarray(L0_ffn_b_con, np.float32))
    dnb_c = np.ascontiguousarray(np.asarray(L0_ffn_down_bias, np.float32))

    in_maps = []
    for core in range(8):
        b, j = core // 4, core % 4
        rows = np.concatenate([np.arange(POS(j, i), POS(j, i) + 256)
                               for i in range(NBLK)])
        in_maps.append({
            "x_own": np.ascontiguousarray(x[b][rows]),
            "wqk": wqk_c, "wv": wv_c, "alibi": al_c,
            "wexp_own": np.ascontiguousarray(wexp_c[j * 128:(j + 1) * 128, :]),
            "wcon_own": np.ascontiguousarray(wcon_c[:, j * 128:(j + 1) * 128]),
            "b_exp": bexp_c, "up_bias": upb_c, "b_con": bcon_c, "dn_bias": dnb_c,
            "qpos": np.ascontiguousarray(rows.astype(np.float32)),
            "upvec_sl": np.ascontiguousarray(upv[0:7, :, j * 512:(j + 1) * 512]),
            "dnvec_sl": np.ascontiguousarray(dnv[0:7, :, j * 512:(j + 1) * 512]),
            "upvec_fm": upvec_fm, "dnvec_fm": dnvec_fm,
        })

    nc = _get_nc()
    res = run_bass_kernel_spmd(nc, in_maps, core_ids=list(range(8)))
    out = np.zeros((B, S, D), np.float32)
    for core in range(8):
        b, j = core // 4, core % 4
        oo = res.results[core]["out_own"]
        for i in range(NBLK):
            p = POS(j, i)
            out[b, p:p + 256] = oo[i * 256:(i + 1) * 256]
    return out


# version 03
# speedup vs baseline: 1.0162x; 1.0162x over previous
import numpy as np
from contextlib import ExitStack

import concourse.bass as bass
import concourse.mybir as mybir
from concourse.bass_utils import run_bass_kernel_spmd

F32 = mybir.dt.float32
BF16 = mybir.dt.bfloat16
I32 = mybir.dt.int32
AF = mybir.ActivationFunctionType
ALU = mybir.AluOpType
AX = mybir.AxisListType

B = 2
S = 8192
D = 1024
DH = 4096
DC = 8
JCN = 32
NBLK = 8
NT = 16
SCALE = 1.0 / 32.0
PEN = -3.2e10

DEBUG = False
import os
PH = int(os.environ.get('BASS_KERNEL_PHASE', '6'))
SUB = int(os.environ.get('BASS_KERNEL_SUB', '9'))


def POS(j, i):
    return (4 * i + (i + j) % 4) * 256


def build_program():
    nc = bass.Bass("TRN2", target_bir_lowering=False, debug=False)

    def din(name, shape, dtype=F32):
        return nc.dram_tensor(name, shape, dtype, kind="ExternalInput").ap()

    x_own = din("x_own", [2048, D])
    wqk = din("wqk", [D, D])
    wv = din("wv", [D, D])
    alibi = din("alibi", [1])
    wexp_own = din("wexp_own", [128, DH])
    wcon_own = din("wcon_own", [DH, 128])
    b_exp_in = din("b_exp", [DH])
    up_bias_in = din("up_bias", [DH])
    b_con_in = din("b_con", [D])
    dn_bias_in = din("dn_bias", [D])
    qpos_in = din("qpos", [2048])
    upvec_sl = din("upvec_sl", [7, 2, 512])
    dnvec_sl = din("dnvec_sl", [7, 2, 512])
    upvec_fm_in = din("upvec_fm", [128, 5 * 2 * JCN])
    dnvec_fm_in = din("dnvec_fm", [128, 5 * 2 * JCN])

    out_own = nc.dram_tensor("out_own", [2048, D], F32, kind="ExternalOutput").ap()
    dbg = {}
    if DEBUG:
        for nm, shp in [("xnfm", [128, DC * 2048]), ("qfm", [128, DC * 2048]),
                        ("extra", [1, 2048]), ("wupc", [128, DH]),
                        ("wdnc", [128, JCN * 128]), ("bmid", [128, JCN]),
                        ("attn", [128, NT * D]), ("xsb", [128, DH]),
                        ("esb", [128, DH])]:
            dbg[nm] = nc.dram_tensor("dbg_" + nm, shp, F32, kind="ExternalOutput").ap()

    ccin_xn = nc.dram_tensor("ccin_xn", [DC * 128, 2048], BF16, kind="Internal").ap()
    xn_all = nc.dram_tensor("xn_all", [4 * DC * 128, 2048], BF16, kind="Internal").ap()
    ccin_v = nc.dram_tensor("ccin_v", [2048, D], BF16, kind="Internal").ap()
    v_all = nc.dram_tensor("v_all", [4 * 2048, D], BF16, kind="Internal").ap()
    ccin_xe = nc.dram_tensor("ccin_xe", [256, 512], BF16, kind="Internal").ap()
    xe_all = nc.dram_tensor("xe_all", [8 * 256, 512], BF16, kind="Internal").ap()
    ccin_wup = nc.dram_tensor("ccin_wup", [128, DH], BF16, kind="Internal").ap()
    wup_all = nc.dram_tensor("wup_all", [8 * 128, DH], BF16, kind="Internal").ap()
    ccin_wdn = nc.dram_tensor("ccin_wdn", [DH, 128], BF16, kind="Internal").ap()
    wdn_all = nc.dram_tensor("wdn_all", [8 * DH, 128], BF16, kind="Internal").ap()

    g4 = [[0, 1, 2, 3], [4, 5, 6, 7]]
    g8 = [list(range(8))]

    main = ExitStack()
    e = main.enter_context
    with main:
        xn_fm = e(nc.sbuf_tensor("xn_fm", [128, DC, 2048], BF16))
        attn_sb = e(nc.sbuf_tensor("attn_sb", [128, NT, D], BF16))
        kpos_f = e(nc.sbuf_tensor("kpos_f", [128, 64], F32))
        alibi_k = e(nc.sbuf_tensor("alibi_k", [128, 64], F32))
        slope_c = e(nc.sbuf_tensor("slope_c", [128, 1], F32))
        ones_f1 = e(nc.sbuf_tensor("ones_f1", [1, 128], F32))
        ones_bc = e(nc.sbuf_tensor("ones_bc", [128, 1], BF16))
        ident_bf = e(nc.sbuf_tensor("ident_bf", [128, 128], BF16))
        ident_f = e(nc.sbuf_tensor("ident_f", [128, 128], F32))
        qpos_bc = e(nc.sbuf_tensor("qpos_bc", [128, NBLK, 256], F32))
        bmid_fm = e(nc.sbuf_tensor("bmid_fm", [128, JCN], F32))
        upv_fm = e(nc.sbuf_tensor("upv_fm", [128, 5 * 2 * JCN], F32))
        dnv_fm = e(nc.sbuf_tensor("dnv_fm", [128, 5 * 2 * JCN], F32))
        iotac_f = e(nc.sbuf_tensor("iotac_f", [128, 1], F32))
        iota4_f = e(nc.sbuf_tensor("iota4_f", [128, 512], F32))

        arena = e(nc.psum_tensor("arena", [128, 4096], F32))

        def A(c0, c1, p0=0, p1=128):
            return arena[p0:p1, c0:c1]
        tpsum = arena[:, 3584:4096].bitcast(BF16)
        tp6 = arena[:, 3072:3584].bitcast(BF16)
        tb0 = arena[:, 0:512].bitcast(BF16)
        tb1 = arena[:, 512:1024].bitcast(BF16)

        sem_names = ("s_ld", "s_x", "s_kv", "s_ws", "s_cc", "s_st", "s_scp",
                     "s_avp", "s_exp", "s_msk", "s_ev", "s_rz", "s_pe", "s_act",
                     "s_dve", "s_pool", "s_hps", "s_gel", "s_yps", "s_ycon",
                     "s_asm", "s_pp_pe", "s_pp_dve", "s_pp_pool", "s_xe", "s_vc", "s_init",
                     "s_xA", "s_xB", "s_kvA", "s_kvB", "s_wsA", "s_wsB",
                     "s_wc", "s_vcA", "s_vcB", "s_stA", "s_stB", "s_xedone")
        SM = {nm: e(nc.semaphore(nm)) for nm in sem_names}
        (s_ld, s_x, s_kv, s_ws, s_cc, s_st, s_scp, s_avp, s_exp, s_msk, s_ev,
         s_rz, s_pe, s_act, s_dve, s_pool, s_hps, s_gel, s_yps, s_ycon, s_asm,
         s_pp_pe, s_pp_dve, s_pp_pool, s_xe, s_vc, s_init, s_xA, s_xB, s_kvA,
         s_kvB, s_wsA, s_wsB, s_wc, s_vcA, s_vcB, s_stA, s_stB, s_xedone) = (
            SM[n] for n in sem_names)
        s_xP = [s_xA, s_xB]
        s_kvP = [s_kvA, s_kvB]
        s_wsP = [s_wsA, s_wsB]
        s_vcP = [s_vcA, s_vcB]
        s_stP = [s_stA, s_stB]

        block = e(nc.Block())

        def sect(ph, engdec):
            if PH >= ph:
                return engdec
            return lambda f: None
        n_ld = [0]

        def colbc(t, coloff, n):
            row = int(np.prod(t.shape[1:]))
            return bass.AP(t, coloff, [[row, 128], [0, n]])


        def pldma(g, out, in_):
            g.dma_start(out=out, in_=in_).then_inc(s_ld, 16)
            n_ld[0] += 16

        sc_qg = ExitStack()
        q_fm = sc_qg.enter_context(nc.sbuf_tensor("q_fm", [128, DC, 2048], BF16))
        extra_row = sc_qg.enter_context(nc.sbuf_tensor("extra_row", [1, 2048], F32))
        wexpT = sc_qg.enter_context(nc.sbuf_tensor("wexpT", [128, JCN, 128], BF16))
        bexp_fm = sc_qg.enter_context(nc.sbuf_tensor("bexp_fm", [128, JCN], BF16))
        upb_fm = sc_qg.enter_context(nc.sbuf_tensor("upb_fm", [128, JCN], F32))

        sc_ln = ExitStack()
        iota_i = sc_ln.enter_context(nc.sbuf_tensor("iota_i", [128, 512], I32))
        x_tile = sc_ln.enter_context(nc.sbuf_tensor("x_tile", [128, 2, D], F32))
        ln_sum = sc_ln.enter_context(nc.sbuf_tensor("ln_sum", [128, NT], F32))
        ln_sq = sc_ln.enter_context(nc.sbuf_tensor("ln_sq", [128, NT], F32))
        rstd = sc_ln.enter_context(nc.sbuf_tensor("rstd", [128, NT], F32))
        nmr = sc_ln.enter_context(nc.sbuf_tensor("nmr", [128, NT], F32))
        var_t = sc_ln.enter_context(nc.sbuf_tensor("var_t", [128, NT], F32))
        std_t = sc_ln.enter_context(nc.sbuf_tensor("std_t", [128, NT], F32))
        m2_t = sc_ln.enter_context(nc.sbuf_tensor("m2_t", [128, NT], F32))
        xn_tm = sc_ln.enter_context(nc.sbuf_tensor("xn_tm", [128, 2, D], BF16))

        sc_wv = ExitStack()
        wv_bf = sc_wv.enter_context(nc.sbuf_tensor("wv_bf", [128, DC, D], BF16))

        sc_vp = ExitStack()
        wvT = sc_vp.enter_context(nc.sbuf_tensor("wvT", [128, DC, D], BF16))
        v_ping = sc_vp.enter_context(nc.sbuf_tensor("v_ping", [128, 2, D], BF16))

        sc_qp = ExitStack()
        wqk_bf = sc_qp.enter_context(nc.sbuf_tensor("wqk_bf", [128, DC, D], BF16))
        q2scr = sc_qp.enter_context(nc.sbuf_tensor("q2scr", [128, 2, 512], BF16))
        rs_c = sc_qp.enter_context(nc.sbuf_tensor("rs_c", [1, 1], F32))

        sc_bx = ExitStack()
        bexp_32 = sc_bx.enter_context(nc.sbuf_tensor("bexp_32", [32, 128], BF16))
        upb_32 = sc_bx.enter_context(nc.sbuf_tensor("upb_32", [32, 128], F32))
        wexp_in = sc_bx.enter_context(nc.sbuf_tensor("wexp_in", [128, DH], BF16))

        @sect(1, block.gpsimd)
        def _(g):
            g.iota(iota_i[:, 128:129], [[0, 1]], channel_multiplier=1)
            g.iota(iota_i[:, 192:256], [[128, 64]], channel_multiplier=1)
            g.drain()
            g.tensor_copy(iotac_f[:, :], iota_i[:, 128:129])
            g.tensor_copy(kpos_f[:, :], iota_i[:, 192:256])
            g.drain()
            g.iota(iota_i[:, 0:512].rearrange("p (a b) -> p a b", b=128),
                   [[0, 4], [1, 128]], channel_multiplier=0)
            g.drain()
            g.tensor_copy(iota4_f[:, :], iota_i[:, 0:512])
            g.memset(ones_f1[:, :], 1.0)
            g.memset(ones_bc[:, :], 1.0)
            g.drain()
            g.engine_nop().then_inc(s_init, 1)
            pldma(g, wqk_bf[:, :, :], wqk.rearrange("(dc p) e -> p dc e", p=128))
            pldma(g, wv_bf[:, :, :], wv.rearrange("(dc p) e -> p dc e", p=128))
            pldma(g, bexp_32[:, :], b_exp_in.rearrange("(a b) -> a b", b=128))
            pldma(g, upb_32[:, :], up_bias_in.rearrange("(a b) -> a b", b=128))
            pldma(g, upv_fm[:, :], upvec_fm_in[:, :])
            pldma(g, dnv_fm[:, :], dnvec_fm_in[:, :])
            pldma(g, slope_c[:, :], bass.AP(alibi.tensor, 0, [[0, 128], [1, 1]]))
            pldma(g, qpos_bc[:, :, :],
                  bass.AP(qpos_in.tensor, 0, [[0, 128], [256, NBLK], [1, 256]]))
            pldma(g, wexp_in[:, :], wexp_own[:, :])
            g.wait_ge(s_ld, n_ld[0])
            g.tensor_tensor(out=alibi_k[:, :], in0=kpos_f[:, :],
                            in1=colbc(slope_c, 0, 64), op=ALU.mult)
            g.drain()
            g.engine_nop().then_inc(s_pool, 1)
        LD_EARLY = n_ld[0]

        @sect(1, block.vector)
        def _(v):
            v.wait_ge(s_init, 1)
            v.tensor_scalar(out=ident_f[:, :], in0=iota4_f[:, 0:128],
                            scalar1=iotac_f[:, 0:1], scalar2=None, op0=ALU.is_equal)
            v.drain()
            v.tensor_copy(ident_bf[:, :], ident_f[:, :])
            v.tensor_scalar(out=iota4_f[:, :], in0=iota4_f[:, :],
                            scalar1=iotac_f[:, 0:1], scalar2=None, op0=ALU.is_equal)
            v.drain()
            v.engine_nop().then_inc(s_init, 1)

        @sect(1, block.sync)
        def _(sy):
            for t in range(NT):
                if t >= 2:
                    sy.wait_ge(s_dve, t - 1)
                    sy.wait_ge(s_act, t - 1)
                sy.dma_start(out=x_tile[:, t % 2, :],
                             in_=x_own[t * 128:(t + 1) * 128, :]).then_inc(s_xP[t % 2], 16)

        @sect(1, block.vector)
        def _(v):
            for t in range(NT):
                v.wait_ge(s_xP[t % 2], 16 * (t // 2 + 1))
                v.reduce_sum(ln_sum[:, t:t + 1], x_tile[:, t % 2, :],
                             axis=AX.X).then_inc(s_dve, 1)
        DVE_L1 = NT

        @sect(1, block.scalar)
        def _(s):
            for t in range(NT):
                s.wait_ge(s_xP[t % 2], 16 * (t // 2 + 1))
                s.activation(out=A(2048, 3072), in_=x_tile[:, t % 2, :], func=AF.Square,
                             accum_out=ln_sq[:, t:t + 1]).then_inc(s_act, 1)
                s.drain()
        ACT_L1 = NT

        @sect(1, block.vector)
        def _(v):
            v.wait_ge(s_dve, DVE_L1)
            v.wait_ge(s_act, ACT_L1)
            v.tensor_scalar(out=ln_sum[:, :], in0=ln_sum[:, :], scalar1=1.0 / D,
                            scalar2=None, op0=ALU.mult)
            v.drain()
            v.tensor_mul(m2_t[:, :], ln_sum[:, :], ln_sum[:, :])
            v.drain()
            v.scalar_tensor_tensor(out=var_t[:, :], in0=ln_sq[:, :], scalar=1.0 / D,
                                   in1=m2_t[:, :], op0=ALU.mult, op1=ALU.subtract)
            v.drain()
            v.tensor_scalar(out=var_t[:, :], in0=var_t[:, :], scalar1=1e-5,
                            scalar2=None, op0=ALU.add).then_inc(s_dve, 1)
        DVE_VAR = DVE_L1 + 1

        @sect(1, block.scalar)
        def _(s):
            s.wait_ge(s_dve, DVE_VAR)
            s.activation(out=std_t[:, :], in_=var_t[:, :], func=AF.Sqrt,
                         bias=0.0, scale=1.0).then_inc(s_act, 1)
        ACT_STD = ACT_L1 + 1

        @sect(1, block.vector)
        def _(v):
            v.wait_ge(s_act, ACT_STD)
            v.reciprocal(rstd[:, :], std_t[:, :])
            v.drain()
            v.tensor_mul(nmr[:, :], ln_sum[:, :], rstd[:, :])
            v.drain()
            v.tensor_scalar(out=nmr[:, :], in0=nmr[:, :], scalar1=-1.0,
                            scalar2=None, op0=ALU.mult).then_inc(s_dve, 1)
        DVE_RSTD = DVE_VAR + 1

        @sect(1, block.sync)
        def _(sy):
            for t in range(NT):
                if t < 2:
                    sy.wait_ge(s_act, ACT_L1 - 1 + t)
                    sy.wait_ge(s_dve, DVE_L1 - 1 + t)
                else:
                    sy.wait_ge(s_act, ACT_STD + t - 1)
                sy.dma_start(out=x_tile[:, t % 2, :],
                             in_=x_own[t * 128:(t + 1) * 128, :]).then_inc(s_xP[t % 2], 16)

        @sect(1, block.scalar)
        def _(s):
            s.wait_ge(s_dve, DVE_RSTD)
            for t in range(NT):
                s.wait_ge(s_xP[t % 2], 16 * (8 + t // 2 + 1))
                if t >= 2:
                    s.wait_ge(s_pe, t - 1)
                s.activation(out=xn_tm[:, t % 2, :], in_=x_tile[:, t % 2, :],
                             func=AF.Identity, bias=nmr[:, t:t + 1],
                             scale=rstd[:, t:t + 1]).then_inc(s_act, 1)
        ACT_XN = ACT_STD + NT

        @sect(1, block.tensor)
        def _(te):
            te.wait_ge(s_pool, 1)
            te.wait_ge(s_init, 2)
            for t in range(NT):
                te.wait_ge(s_act, ACT_STD + t + 1)
                if t >= 1:
                    te.wait_ge(s_dve, DVE_RSTD + t)
                for dc in range(DC):
                    ins = te.transpose(tpsum[:, dc * 128:(dc + 1) * 128],
                                       xn_tm[:, t % 2, dc * 128:(dc + 1) * 128],
                                       ident_bf[:, :])
                ins.then_inc(s_pe, 1)
        PE_XNT = NT

        @sect(1, block.vector)
        def _(v):
            for t in range(NT):
                v.wait_ge(s_pe, t + 1)
                for dc in range(DC):
                    ins = v.tensor_copy(xn_fm[:, dc, t * 128:(t + 1) * 128],
                                        tpsum[:, dc * 128:(dc + 1) * 128])
                ins.then_inc(s_dve, 1)
        DVE_XNF = DVE_RSTD + NT

        @sect(1, block.gpsimd)
        def _(g):
            g.wait_ge(s_dve, DVE_XNF)
            g.dma_start(out=ccin_xn.rearrange("(dc p) t -> p dc t", p=128),
                        in_=xn_fm[:, :, :]).then_inc(s_ld, 16)
            n_ld[0] += 16
            g.wait_ge(s_ld, n_ld[0])
            g.collective_compute("AllGather", ALU.bypass, replica_groups=g4,
                                 ins=[ccin_xn[:, :].opt()],
                                 outs=[xn_all[:, :].opt()]).then_inc(s_cc, 1)

        @sect(2, block.tensor)
        def _(te):
            te.wait_ge(s_ld, LD_EARLY)
            te.wait_ge(s_dve, DVE_XNF)
            n = 0
            for dc in range(DC):
                for ec in range(DC):
                    if n >= 2:
                        te.wait_ge(s_dve, DVE_XNF + n - 1)
                    tp = tpsum if n % 2 == 0 else tp6
                    ins = te.transpose(tp[:, 0:128],
                                       wv_bf[:, ec, dc * 128:(dc + 1) * 128],
                                       ident_bf[:, :])
                    ins.then_inc(s_pe, 1)
                    n += 1
        PE_WVT = PE_XNT + 64

        @sect(2, block.vector)
        def _(v):
            n = 0
            for dc in range(DC):
                for ec in range(DC):
                    v.wait_ge(s_pe, PE_XNT + n + 1)
                    tp = tpsum if n % 2 == 0 else tp6
                    ins = v.tensor_copy(wvT[:, dc, ec * 128:(ec + 1) * 128],
                                        tp[:, 0:128])
                    ins.then_inc(s_dve, 1)
                    n += 1
        DVE_WVT = DVE_XNF + 64

        @sect(2, block.tensor)
        def _(te):
            te.wait_ge(s_dve, DVE_WVT)
            for t in range(NT):
                if t >= 2:
                    te.wait_ge(s_act, ACT_XN + t - 1)
                ps = A(0, 1024) if t % 2 == 0 else A(1024, 2048)
                for dc in range(DC):
                    for h in range(2):
                        ins = te.matmul(ps[:, h * 512:(h + 1) * 512],
                                        xn_fm[:, dc, t * 128:(t + 1) * 128],
                                        wvT[:, dc, h * 512:(h + 1) * 512],
                                        start=(dc == 0), stop=(dc == DC - 1))
                ins.then_inc(s_pe, 1)
        PE_VP = PE_WVT + NT

        @sect(2, block.scalar)
        def _(s):
            for t in range(NT):
                s.wait_ge(s_pe, PE_WVT + t + 1)
                if t >= 2:
                    s.wait_ge(s_vcP[t % 2], 16 * (t // 2))
                ps = A(0, 1024) if t % 2 == 0 else A(1024, 2048)
                s.copy(v_ping[:, t % 2, :], ps[:, :]).then_inc(s_act, 1)
        ACT_VP = ACT_XN + NT

        @sect(2, block.sync)
        def _(sy):
            for t in range(NT):
                sy.wait_ge(s_act, ACT_XN + t + 1)
                sy.dma_start(out=ccin_v[t * 128:(t + 1) * 128, :],
                             in_=v_ping[:, t % 2, :]).then_inc(s_vcP[t % 2], 16)

        @sect(2, block.gpsimd)
        def _(g):
            g.wait_ge(s_vcA, 16 * (NT // 2))
            g.wait_ge(s_vcB, 16 * (NT // 2))
            g.collective_compute("AllGather", ALU.bypass, replica_groups=g4,
                                 ins=[ccin_v[:, :].opt()],
                                 outs=[v_all[:, :].opt()]).then_inc(s_cc, 1)

        @sect(2, block.tensor)
        def _(te):
            te.wait_ge(s_act, ACT_VP)
            n = 0
            for et in range(DC):
                for tb in range(4):
                    if n >= 2:
                        te.wait_ge(s_act, ACT_VP + n - 1)
                    ps = A(0, 512) if n % 2 == 0 else A(512, 1024)
                    for dc in range(DC):
                        ins = te.matmul(ps[:, :],
                                        wqk_bf[:, dc, et * 128:(et + 1) * 128],
                                        xn_fm[:, dc, tb * 512:(tb + 1) * 512],
                                        start=(dc == 0), stop=(dc == DC - 1))
                    ins.then_inc(s_pe, 1)
                    n += 1
        PE_QP = PE_VP + 32

        @sect(2, block.scalar)
        def _(s):
            n = 0
            for et in range(DC):
                for tb in range(4):
                    s.wait_ge(s_pe, PE_VP + n + 1)
                    ps = A(0, 512) if n % 2 == 0 else A(512, 1024)
                    s.copy(q_fm[:, et, tb * 512:(tb + 1) * 512], ps[:, :]).then_inc(s_act, 1)
                    n += 1
        ACT_QP = ACT_VP + 32

        @sect(2, block.scalar)
        def _(s):
            n = 0
            for tb in range(4):
                for et in range(DC):
                    s.wait_ge(s_act, ACT_VP + et * 4 + tb + 1)
                    if n >= 2:
                        s.wait_ge(s_pe, PE_QP + n - 1)
                    s.activation(out=q2scr[:, n % 2, :],
                                 in_=q_fm[:, et, tb * 512:(tb + 1) * 512],
                                 func=AF.Square).then_inc(s_act, 1)
                    n += 1
        ACT_Q2 = ACT_QP + 32

        @sect(2, block.tensor)
        def _(te):
            n = 0
            for tb in range(4):
                for et in range(DC):
                    te.wait_ge(s_act, ACT_QP + n + 1)
                    ins = te.matmul(A(1024 + tb * 512, 1024 + (tb + 1) * 512, 0, 1),
                                    ones_bc[:, :], q2scr[:, n % 2, :],
                                    start=(et == 0), stop=(et == DC - 1))
                    ins.then_inc(s_pe, 1)
                    n += 1
        PE_Q2 = PE_QP + 32

        @sect(2, block.scalar)
        def _(s):
            s.wait_ge(s_pe, PE_Q2)
            s.activation(out=extra_row[:, :], in_=A(1024, 3072, 0, 1), func=AF.Sqrt,
                         bias=0.0, scale=1.0).then_inc(s_act, 1)
            s.activation(out=rs_c[:, :], in_=slope_c[0:1, 0:1], func=AF.Relu,
                         bias=0.0, scale=1.0).then_inc(s_act, 1)
        ACT_BND = ACT_Q2 + 2

        @sect(2, block.vector)
        def _(v):
            v.wait_ge(s_act, ACT_BND)
            v.tensor_scalar(out=extra_row[:, :], in0=extra_row[:, :], scalar1=1.01,
                            scalar2=1.0, op0=ALU.mult, op1=ALU.add)
            v.drain()
            v.scalar_tensor_tensor(out=extra_row[:, :],
                                   in0=qpos_bc[0:1, :, :].rearrange("p a b -> p (a b)"),
                                   scalar=rs_c[0:1, 0:1], in1=extra_row[:, :],
                                   op0=ALU.mult, op1=ALU.add)
            v.drain()
            v.tensor_scalar(out=extra_row[:, :], in0=extra_row[:, :],
                            scalar1=-1.0 / SCALE, scalar2=None,
                            op0=ALU.mult).then_inc(s_dve, 1)
        DVE_EXT = DVE_WVT + 1

        @sect(3 if SUB >= 1 else 99, block.tensor)
        def _(te):
            te.wait_ge(s_ld, LD_EARLY)
            for c in range(JCN):
                if c >= 2:
                    te.wait_ge(s_pp_dve, c - 1)
                tp = tpsum if c % 2 == 0 else tp6
                ins = te.transpose(tp[:, 0:128],
                                   wexp_in[:, c * 128:(c + 1) * 128], ident_bf[:, :])
                ins.then_inc(s_pp_pe, 1)
            te.wait_ge(s_pp_dve, JCN - 1)
            ins = te.transpose(tpsum[:, 0:32], bexp_32[:, :], ident_bf[0:32, 0:32])
            ins.then_inc(s_pp_pe, 1)
            te.wait_ge(s_act, ACT_BND)
            ins = te.transpose(A(2432, 2464), upb_32[:, :], ident_f[0:32, 0:32])
            ins.then_inc(s_pp_pe, 1)
        PP_PE_T0 = JCN + 2

        @sect(3 if SUB >= 1 else 99, block.vector)
        def _(v):
            for c in range(JCN):
                v.wait_ge(s_pp_pe, c + 1)
                tp = tpsum if c % 2 == 0 else tp6
                ins = v.tensor_copy(wexpT[:, c, :], tp[:, 0:128])
                ins.then_inc(s_pp_dve, 1)
            v.wait_ge(s_pp_pe, JCN + 1)
            ins = v.tensor_copy(bexp_fm[:, :], tpsum[:, 0:32])
            ins.then_inc(s_pp_dve, 1)
            v.wait_ge(s_pp_pe, JCN + 2)
            ins = v.tensor_copy(upb_fm[:, :], A(2432, 2464))
            ins.then_inc(s_pp_dve, 1)
        PP_DVE_T0 = JCN + 2

        sc_bx.close()
        sc_qp.close()
        sc_vp.close()
        sc_wv.close()
        sc_ln.close()

        sc_pp = ExitStack()
        app = sc_pp.enter_context
        wcon_in = app(nc.sbuf_tensor("wcon_in", [128, JCN, 128], BF16))
        z_a = app(nc.sbuf_tensor("z_a", [128, JCN, 128], BF16))
        z_t = app(nc.sbuf_tensor("z_t", [128, 16, 128], BF16))
        zd_a = app(nc.sbuf_tensor("zd_a", [128, JCN, 128], BF16))
        b1_fm = app(nc.sbuf_tensor("b1_fm", [128, JCN], F32))
        b2_fm = app(nc.sbuf_tensor("b2_fm", [128, JCN], F32))
        bt_f = app(nc.sbuf_tensor("bt_f", [128, JCN], F32))
        X_sb = app(nc.sbuf_tensor("X_sb", [128, 8, 512], BF16))
        E_sb = app(nc.sbuf_tensor("E_sb", [128, 8, 512], BF16))
        wupc = app(nc.sbuf_tensor("wupc", [128, DH], BF16))
        wdnc = app(nc.sbuf_tensor("wdnc", [128, JCN, 128], BF16))

        sc_xe = ExitStack()
        xe_p = sc_xe.enter_context(nc.sbuf_tensor("xe_p", [128, 2, 512], F32))
        xe_m = sc_xe.enter_context(nc.sbuf_tensor("xe_m", [128, 4, 256], F32))
        xe_w = sc_xe.enter_context(nc.sbuf_tensor("xe_w", [128, 2, 512], F32))
        xe_bf = sc_xe.enter_context(nc.sbuf_tensor("xe_bf", [128, 512], BF16))

        up_seq = [z_a, wexpT, z_a, wexpT, z_a, wexpT]
        dn_seq = [wcon_in, zd_a, wcon_in, zd_a, wcon_in, zd_a]
        Z2_FINAL = up_seq[5]
        ZD_FINAL = dn_seq[5]

        @sect(3, block.gpsimd)
        def _(g):
            g.wait_ge(s_pe, PE_Q2)
            g.wait_ge(s_pp_pe, PP_PE_T0)
            pldma(g, wcon_in[:, :, :], wcon_own.rearrange("(c p) d -> p c d", p=128))
            g.engine_nop().then_inc(s_pool, 1)
        LD_WEXP = n_ld[0]

        def stage_cols(g, vec_sl, sg, transposed, cur):
            sig = 1 << sg
            pldma(g, xe_w[:, 0, :], bass.AP(vec_sl.tensor, (sg * 2 + 0) * 512,
                                            [[0, 128], [1, 512]]))
            pldma(g, xe_w[:, 1, :], bass.AP(vec_sl.tensor, (sg * 2 + 1) * 512,
                                            [[0, 128], [1, 512]]))
            g.wait_ge(s_ld, n_ld[0])
            nb = 512 // (2 * sig)

            def vw(apobj, off):
                return bass.AP(apobj.tensor, apobj.offset + off,
                               [list(apobj.ap[0]), [2 * sig, nb], [1, sig]])

            def mw(k):
                return bass.AP(xe_m, k * 256, [[1024, 128], [sig, nb], [1, sig]])
            src = xe_p[:, cur, :]
            dst = xe_p[:, 1 - cur, :]
            a_in, b_in = vw(src, 0), vw(src, sig)
            dg_a, dg_b = vw(xe_w[:, 0, :], 0), vw(xe_w[:, 0, :], sig)
            of_a, of_b = vw(xe_w[:, 1, :], 0), vw(xe_w[:, 1, :], sig)
            na, nb_ = vw(dst, 0), vw(dst, sig)
            g.tensor_mul(mw(0), a_in, dg_a)
            g.tensor_mul(mw(1), b_in, of_b if transposed else of_a)
            g.tensor_mul(mw(2), b_in, dg_b)
            g.tensor_mul(mw(3), a_in, of_a if transposed else of_b)
            g.drain()
            g.tensor_add(na, mw(0), mw(1))
            g.tensor_add(nb_, mw(2), mw(3))
            g.drain()

        @sect(3 if SUB >= 2 else 99, block.gpsimd)
        def _(g):
            g.wait_ge(s_pool, 1)
            g.wait_ge(s_init, 2)
            g.wait_ge(s_pe, PE_Q2)
            g.wait_ge(s_pp_pe, PP_PE_T0)
            g.wait_ge(s_pp_dve, PP_DVE_T0)
            for half, (vec_sl, stages, tr) in enumerate(
                    [(upvec_sl, list(range(7)), False),
                     (dnvec_sl, list(range(6, -1, -1)), True)]):
                g.tensor_copy(xe_p[:, 0, :], iota4_f[:, :])
                g.drain()
                cur = 0
                for sg in stages:
                    stage_cols(g, vec_sl, sg, tr, cur)
                    cur = 1 - cur
                g.tensor_copy(xe_bf[:, :], xe_p[:, cur, :])
                g.drain()
                g.dma_start(out=ccin_xe[half * 128:(half + 1) * 128, :],
                            in_=xe_bf[:, :]).then_inc(s_ld, 16)
                n_ld[0] += 16
            g.wait_ge(s_ld, n_ld[0])
            g.engine_nop().then_inc(s_xedone, 1)
            g.collective_compute("AllGather", ALU.bypass, replica_groups=g8,
                                 ins=[ccin_xe[:, :].opt()],
                                 outs=[xe_all[:, :].opt()]).then_inc(s_cc, 1)
            g.wait_ge(s_cc, 3)
            g.wait_ge(s_pp_pe, JCN + 2)
            g.dma_start(out=X_sb[:, :, :],
                        in_=xe_all.rearrange("(r h p) c -> h p r c", r=8, p=128)[0]
                        ).then_inc(s_xe, 16)
            g.dma_start(out=E_sb[:, :, :],
                        in_=xe_all.rearrange("(r h p) c -> h p r c", r=8, p=128)[1]
                        ).then_inc(s_xe, 16)

        def hi_stage(g, s_idx, src, dst, vecs_fm, transposed):
            sc = 1 << (s_idx - 7)

            def cvec(t, c):
                return colbc(vecs_fm, ((s_idx - 7) * 2 + t) * JCN + c, 128)
            pairs = [(c, c ^ sc) for c in range(JCN) if not (c & sc)]
            for q in range(4):
                sub = pairs[q * 4:(q + 1) * 4]
                for idx, (ca, cb) in enumerate(sub):
                    o_a = cvec(1, cb) if transposed else cvec(1, ca)
                    o_b = cvec(1, ca) if transposed else cvec(1, cb)
                    g.tensor_mul(z_t[:, 4 * idx + 0, :], src[:, ca, :], cvec(0, ca))
                    g.tensor_mul(z_t[:, 4 * idx + 1, :], src[:, cb, :], o_a)
                    g.tensor_mul(z_t[:, 4 * idx + 2, :], src[:, cb, :], cvec(0, cb))
                    g.tensor_mul(z_t[:, 4 * idx + 3, :], src[:, ca, :], o_b)
                g.drain()
                for idx, (ca, cb) in enumerate(sub):
                    g.tensor_add(dst[:, ca, :], z_t[:, 4 * idx + 0, :],
                                 z_t[:, 4 * idx + 1, :])
                    g.tensor_add(dst[:, cb, :], z_t[:, 4 * idx + 2, :],
                                 z_t[:, 4 * idx + 3, :])
                g.drain()

        @sect(3 if SUB >= 3 else 99, block.gpsimd)
        def _(g):
            for k, s_idx in enumerate([11, 10, 9, 8, 7]):
                hi_stage(g, s_idx, dn_seq[k], dn_seq[k + 1], dnv_fm, transposed=True)
            g.engine_nop().then_inc(s_pp_pool, 1)

        sc_xe.close()
        sc_att = ExitStack()
        kv_k = sc_att.enter_context(nc.sbuf_tensor("kv_k", [128, 2, DC, 256], BF16))
        kv_v = sc_att.enter_context(nc.sbuf_tensor("kv_v", [128, 2, 2, D], BF16))
        pt_sb = sc_att.enter_context(nc.sbuf_tensor("pt_sb", [128, 4, 256], BF16))
        pen_sb = sc_att.enter_context(nc.sbuf_tensor("pen_sb", [128, 256], BF16))
        rz_sb = sc_att.enter_context(nc.sbuf_tensor("rz_sb", [128, 2], F32))

        seg_end_kt = []
        _g = 0
        for i in range(NBLK):
            for sg in range(4 * (i + 1)):
                seg_end_kt.append(_g + (sg + 1) * 2)
            _g += 8 * (i + 1)
        TOT_KT = _g

        def av_slice(sub):
            return A(0, 1024) if sub == 0 else A(1024, 2048)

        @sect(4, block.sync)
        def _(sy):
            sy.wait_ge(s_cc, 2)
            sy.wait_ge(s_pe, PE_Q2)
            sy.wait_ge(s_pp_pe, PP_PE_T0)
            sy.wait_ge(s_pp_dve, PP_DVE_T0)
            sy.wait_ge(s_xedone, 1)
            g_seg = 0
            for i in range(NBLK):
                for sg in range(4 * (i + 1)):
                    if g_seg >= 2:
                        sy.wait_ge(s_avp, seg_end_kt[g_seg - 2])
                    par = g_seg % 2
                    i_star, sig = sg // 4, sg % 4
                    r, o = (sig - i_star) % 4, i_star * 256
                    sy.dma_start(
                        out=kv_k[:, par, :, :],
                        in_=xn_all.rearrange("(r dc p) t -> r p dc t", r=4, dc=8)
                        [r, :, :, o:o + 256]).then_inc(s_kvP[par], 16)
                    sy.dma_start(
                        out=kv_v[:, par, :, :],
                        in_=v_all.rearrange("(r kt p) e -> r p kt e", r=4, p=128)
                        [r, :, o // 128:o // 128 + 2, :]).then_inc(s_kvP[par], 16)
                    g_seg += 1

        @sect(4, block.tensor)
        def _(te):
            te.wait_ge(s_dve, DVE_EXT)
            te.wait_ge(s_act, ACT_QP)
            n_kt = 0
            g_seg = 0
            for i in range(NBLK):
                if i >= 1:
                    te.wait_ge(s_ev, i)
                if i == 1:
                    te.wait_ge(s_pp_dve, PP_DVE_T0 + JCN + 1)
                if i == 3:
                    te.wait_ge(s_pp_dve, PP_DVE_T0 + 2 * JCN + 1)
                if i == 5:
                    te.wait_ge(s_pp_dve, PP_DVE_T0 + 3 * JCN + 1)
                nkt_blk = 8 * (i + 1)

                def emit_av(kt_loc, n_av):
                    te.wait_ge(s_exp, n_av + 1)
                    par_av = (g_seg0 + kt_loc // 8) % 2
                    kk_av = (kt_loc // 2) % 4 * 0 + (kt_loc % 2)
                    sgl = kt_loc // 2
                    par_av = (g_seg0 + sgl) % 2
                    first = (kt_loc == 0)
                    last = (kt_loc == nkt_blk - 1)
                    for sub in range(2):
                        lt = pt_sb[:, n_av % 4, sub * 128:(sub + 1) * 128]
                        ins = te.matmul(av_slice(sub)[:, 0:512], lt,
                                        kv_v[:, par_av, kk_av, 0:512],
                                        start=first, stop=last)
                        ins = te.matmul(av_slice(sub)[:, 512:1024], lt,
                                        kv_v[:, par_av, kk_av, 512:1024],
                                        start=first, stop=last)
                        ins = te.matmul(A(2048 + 512 * sub, 2049 + 512 * sub), lt,
                                        ones_bc[:, :], start=first, stop=last)
                    ins.then_inc(s_avp, 1)

                g_seg0 = g_seg
                for sg in range(4 * (i + 1)):
                    par = g_seg % 2
                    te.wait_ge(s_kvP[par], 32 * (g_seg // 2 + 1))
                    for kk in range(2):
                        ktg = sg * 2 + kk
                        sc = A(3072, 3328) if n_kt % 2 == 0 else A(3584, 3840)
                        if n_kt >= 2:
                            te.wait_ge(s_exp, n_kt - 1)
                        for dc in range(DC):
                            ins = te.matmul(sc[:, :],
                                            kv_k[:, par, dc, kk * 128:(kk + 1) * 128],
                                            q_fm[:, dc, i * 256:(i + 1) * 256],
                                            start=(dc == 0), stop=False)
                        ins = te.matmul(sc[:, :], ones_f1[:, :],
                                        extra_row[:, i * 256:(i + 1) * 256],
                                        start=False, stop=True)
                        ins.then_inc(s_scp, 1)
                        if ktg >= 1:
                            emit_av(ktg - 1, n_kt - 1)
                        n_kt += 1
                    g_seg += 1
                emit_av(nkt_blk - 1, n_kt - 1)
                if i == 0:
                    te.wait_ge(s_ev, i + 1)
                    te.wait_ge(s_xe, 32)
                    te.wait_ge(s_pp_dve, PP_DVE_T0)
                    for c in range(JCN):
                        if c >= 2:
                            te.wait_ge(s_pp_dve, PP_DVE_T0 + c - 1)
                        Xc = X_sb[:, c // 4, (c % 4) * 128:(c % 4) * 128 + 128]
                        ins = te.matmul(A(2432, 2560) if c % 2 == 0 else A(2688, 2816),
                                        Xc, wexpT[:, c, :], start=True, stop=True)
                        ins.then_inc(s_pp_pe, 1)
                    te.wait_ge(s_pp_dve, PP_DVE_T0 + JCN)
                    for c in range(JCN):
                        Xc = X_sb[:, c // 4, (c % 4) * 128:(c % 4) * 128 + 128]
                        ins = te.matmul(A(2576 + c, 2577 + c), Xc,
                                        bexp_fm[:, c:c + 1], start=True, stop=True)
                    ins.then_inc(s_pp_pe, 1)
                if i == 2:
                    te.wait_ge(s_ev, i + 1)
                    te.wait_ge(s_pp_pool, 1)
                    te.wait_ge(s_xe, 32)
                    te.wait_ge(s_pp_dve, PP_DVE_T0 + JCN + 1)
                    for c in range(JCN):
                        if c >= 2:
                            te.wait_ge(s_pp_dve, PP_DVE_T0 + JCN + 1 + c - 1)
                        Ec = E_sb[:, c // 4, (c % 4) * 128:(c % 4) * 128 + 128]
                        ins = te.matmul(A(2432, 2560) if c % 2 == 0 else A(2688, 2816),
                                        Ec, ZD_FINAL[:, c, :], start=True, stop=True)
                        ins.then_inc(s_pp_pe, 1)
                if i == 4:
                    te.wait_ge(s_ev, i + 1)
                    te.wait_ge(s_pp_pool, 2)
                    for c in range(JCN):
                        if c >= 2:
                            te.wait_ge(s_pp_dve, PP_DVE_T0 + 2 * JCN + 1 + c - 1)
                        tb = tb0 if c % 2 == 0 else tb1
                        ins = te.transpose(tb[:, 0:128],
                                           Z2_FINAL[:, c, :], ident_bf[:, :])
                        ins.then_inc(s_pp_pe, 1)

        @sect(4, block.vector)
        def _(v):
            v.wait_ge(s_ld, LD_EARLY)
            n_kt = 0
            n_msk = 0
            for i in range(NBLK):
                for sg in range(4 * (i + 1)):
                    for kk in range(2):
                        ktg = sg * 2 + kk
                        if ktg >= 8 * i:
                            sc = A(3072, 3328) if n_kt % 2 == 0 else A(3584, 3840)
                            v.wait_ge(s_scp, n_kt + 1)
                            v.tensor_scalar(out=pen_sb[:, :], in0=qpos_bc[:, i, :],
                                            scalar1=kpos_f[:, ktg:ktg + 1],
                                            scalar2=PEN, op0=ALU.is_lt, op1=ALU.mult)
                            v.drain()
                            ins = v.tensor_add(sc[:, :], sc[:, :], pen_sb[:, :])
                            ins.then_inc(s_msk, 1)
                            v.drain()
                            n_msk += 1
                        n_kt += 1
                v.wait_ge(s_avp, n_kt)
                v.reciprocal(rz_sb[:, :], bass.AP(arena, 2048, [[4096, 128], [512, 2]]))
                v.drain()
                v.engine_nop().then_inc(s_rz, 1)
                if i == 0:
                    for c in range(JCN):
                        v.wait_ge(s_pp_pe, PP_PE_T0 + c + 1)
                        ps = A(2432, 2560) if c % 2 == 0 else A(2688, 2816)
                        ins = v.tensor_copy(z_a[:, c, :], ps[:, :])
                        ins.then_inc(s_pp_dve, 1)
                    v.wait_ge(s_pp_pe, PP_PE_T0 + JCN + 1)
                    ins = v.tensor_copy(b1_fm[:, :], A(2576, 2608))
                    ins.then_inc(s_pp_dve, 1)
                if i == 2:
                    for c in range(JCN):
                        v.wait_ge(s_pp_pe, PP_PE_T0 + JCN + 1 + c + 1)
                        ps = A(2432, 2560) if c % 2 == 0 else A(2688, 2816)
                        ins = v.tensor_copy(wdnc[:, c, :], ps[:, :])
                        ins.then_inc(s_pp_dve, 1)
                if i == 4:
                    for c in range(JCN):
                        v.wait_ge(s_pp_pe, PP_PE_T0 + 2 * JCN + 1 + c + 1)
                        tb = tb0 if c % 2 == 0 else tb1
                        ins = v.tensor_copy(wupc[:, c * 128:(c + 1) * 128],
                                            tb[:, 0:128])
                        ins.then_inc(s_pp_dve, 1)

        @sect(4, block.scalar)
        def _(s):
            s.wait_ge(s_pool, 1)
            n_kt = 0
            n_msk = 0
            for i in range(NBLK):
                for sg in range(4 * (i + 1)):
                    for kk in range(2):
                        ktg = sg * 2 + kk
                        sc = A(3072, 3328) if n_kt % 2 == 0 else A(3584, 3840)
                        s.wait_ge(s_scp, n_kt + 1)
                        if ktg >= 8 * i:
                            n_msk += 1
                            s.wait_ge(s_msk, n_msk)
                        if n_kt >= 4:
                            s.wait_ge(s_avp, n_kt - 3)
                        ins = s.activation(out=pt_sb[:, n_kt % 4, :], in_=sc[:, :],
                                           func=AF.Exp, bias=alibi_k[:, ktg:ktg + 1],
                                           scale=SCALE)
                        ins.then_inc(s_exp, 1)
                        n_kt += 1
                s.wait_ge(s_rz, i + 1)
                for sub in range(2):
                    ins = s.activation(out=attn_sb[:, 2 * i + sub, :],
                                       in_=av_slice(sub), func=AF.Identity,
                                       bias=0.0, scale=rz_sb[:, sub:sub + 1])
                ins.then_inc(s_ev, 1)

        @sect(4, block.gpsimd)
        def _(g):
            g.wait_ge(s_pp_dve, PP_DVE_T0 + JCN + 1)
            for k, s_idx in enumerate([7, 8, 9, 10, 11]):
                hi_stage(g, s_idx, up_seq[k], up_seq[k + 1], upv_fm, transposed=False)
            bb = [b1_fm, b2_fm]
            bcur = 0
            for s_idx in [7, 8, 9, 10, 11]:
                sc_ = 1 << (s_idx - 7)

                def col(t, c):
                    base = ((s_idx - 7) * 2 + t) * JCN + c
                    return upv_fm[:, base:base + 1]
                pairs = [(c, c ^ sc_) for c in range(JCN) if not (c & sc_)]
                for ca, cb in pairs:
                    g.tensor_mul(bt_f[:, ca:ca + 1], bb[bcur][:, cb:cb + 1], col(1, ca))
                    g.tensor_mul(bt_f[:, cb:cb + 1], bb[bcur][:, ca:ca + 1], col(1, cb))
                g.drain()
                for ca, cb in pairs:
                    g.tensor_mul(bb[1 - bcur][:, ca:ca + 1], bb[bcur][:, ca:ca + 1],
                                 col(0, ca))
                    g.tensor_mul(bb[1 - bcur][:, cb:cb + 1], bb[bcur][:, cb:cb + 1],
                                 col(0, cb))
                g.drain()
                for ca, cb in pairs:
                    g.tensor_add(bb[1 - bcur][:, ca:ca + 1], bb[1 - bcur][:, ca:ca + 1],
                                 bt_f[:, ca:ca + 1])
                    g.tensor_add(bb[1 - bcur][:, cb:cb + 1], bb[1 - bcur][:, cb:cb + 1],
                                 bt_f[:, cb:cb + 1])
                g.drain()
                bcur = 1 - bcur
            g.wait_ge(s_pp_dve, PP_DVE_T0)
            g.tensor_add(bmid_fm[:, :], bb[bcur][:, :], upb_fm[:, :])
            g.drain()
            g.engine_nop().then_inc(s_pp_pool, 1)
            g.engine_nop().then_inc(s_pool, 1)
            g.wait_ge(s_pp_dve, PP_DVE_T0 + JCN + 1 + JCN)
            g.dma_start(out=ccin_wdn.rearrange("(c p) d -> p c d", p=128),
                        in_=wdnc[:, :, :]).then_inc(s_ld, 16)
            n_ld[0] += 16
            g.wait_ge(s_ld, n_ld[0])
            g.collective_compute("AllGather", ALU.bypass, replica_groups=g8,
                                 ins=[ccin_wdn[:, :].opt()],
                                 outs=[wdn_all[:, :].opt()]).then_inc(s_cc, 1)
            g.wait_ge(s_pp_dve, PP_DVE_T0 + 2 * JCN + 1 + JCN)
            g.dma_start(out=ccin_wup[:, :], in_=wupc[:, :]).then_inc(s_ld, 16)
            n_ld[0] += 16
            g.wait_ge(s_ld, n_ld[0])
            g.collective_compute("AllGather", ALU.bypass, replica_groups=g8,
                                 ins=[ccin_wup[:, :].opt()],
                                 outs=[wup_all[:, :].opt()]).then_inc(s_cc, 1)

        if DEBUG:
            @block.gpsimd
            def _(g):
                def dump(o, i_):
                    g.dma_start(out=o, in_=i_).then_inc(s_ld, 16)
                    n_ld[0] += 16
                g.wait_ge(s_dve, DVE_XNF)
                dump(dbg["xnfm"][:, :], xn_fm.rearrange("p dc t -> p (dc t)"))
                g.wait_ge(s_act, ACT_QP)
                dump(dbg["qfm"][:, :], q_fm.rearrange("p dc t -> p (dc t)"))
                g.wait_ge(s_dve, DVE_EXT)
                dump(dbg["extra"][:, :], extra_row[:, :])
                g.wait_ge(s_xe, 32)
                dump(dbg["xsb"][:, :], X_sb.rearrange("p r c -> p (r c)"))
                dump(dbg["esb"][:, :], E_sb.rearrange("p r c -> p (r c)"))
                g.wait_ge(s_pp_dve, PP_DVE_T0 + JCN + 1 + JCN)
                dump(dbg["wdnc"][:, :], wdnc.rearrange("p c d -> p (c d)"))
                g.wait_ge(s_pp_dve, PP_DVE_T0 + 2 * JCN + 1 + JCN)
                dump(dbg["wupc"][:, :], wupc[:, :])
                g.wait_ge(s_pool, 3)
                dump(dbg["bmid"][:, :], bmid_fm[:, :])
                g.wait_ge(s_ev, NBLK)
                dump(dbg["attn"][:, :], attn_sb.rearrange("p t e -> p (t e)"))

        if not DEBUG:
            sc_att.close()
            sc_pp.close()
            sc_qg.close()
        if PH >= 6:
            hg = e(nc.sbuf_tensor("hg", [128, 16, 1024], BF16))
            bcon_row = e(nc.sbuf_tensor("bcon_row", [1, D], F32))
            dnb_row = e(nc.sbuf_tensor("dnb_row", [1, D], F32))
            y1 = e(nc.sbuf_tensor("y1", [128, 8, D], BF16))
            wup_buf = e(nc.sbuf_tensor("wup_buf", [128, 2, DC, 128], BF16))
            wcon_buf = e(nc.sbuf_tensor("wcon_buf", [128, 16, D], BF16))
            x_re = e(nc.sbuf_tensor("x_re", [128, 2, D], F32))
            outb = e(nc.sbuf_tensor("outb", [128, 2, D], F32))
            t1_sb = e(nc.sbuf_tensor("t1_sb", [128, 2, D], F32))

        if PH < 6:
            @block.gpsimd
            def _(g):
                if PH >= 1:
                    g.wait_ge(s_cc, 1)
                    g.wait_ge(s_dve, DVE_XNF)
                if PH >= 2:
                    g.wait_ge(s_cc, 2)
                    g.wait_ge(s_dve, DVE_EXT)
                    g.wait_ge(s_act, ACT_BND)
                if PH >= 3:
                    if SUB >= 1:
                        g.wait_ge(s_pp_dve, PP_DVE_T0)
                    if SUB >= 2:
                        g.wait_ge(s_xe, 32)
                    if SUB >= 3:
                        g.wait_ge(s_pp_pool, 1)
                if PH >= 4:
                    g.wait_ge(s_ev, NBLK)
                if PH >= 5:
                    g.wait_ge(s_cc, 5)
                    g.wait_ge(s_pool, 3)
                for t in range(NT):
                    g.dma_start(out=out_own[t * 128:(t + 1) * 128, :],
                                in_=attn_sb[:, t, :]).then_inc(s_st, 16)
                g.wait_ge(s_st, 16 * 16)

        @sect(6, block.sync)
        def _(sy):
            sy.wait_ge(s_cc, 5)
            sy.wait_ge(s_ev, NBLK)
            sy.dma_start(out=bcon_row[:, :], in_=b_con_in[None, :]).then_inc(s_wc, 16)
            sy.dma_start(out=dnb_row[:, :], in_=dn_bias_in[None, :]).then_inc(s_wc, 16)
            nxr = 0
            for th in range(2):
                for hf in range(2):
                    u = th * 2 + hf
                    for jtl in range(16):
                        jt = hf * 16 + jtl
                        gjt = u * 16 + jtl
                        if gjt >= 2:
                            sy.wait_ge(s_hps, gjt - 1)
                        sy.dma_start(
                            out=wup_buf[:, gjt % 2, :, :],
                            in_=wup_all.rearrange("(dc p) (jt c) -> jt p dc c",
                                                  p=128, c=128)[jt]).then_inc(s_wsP[gjt % 2], 16)
                    if u >= 1:
                        sy.wait_ge(s_yps, u * 8)
                    for jcl in range(16):
                        jc = hf * 16 + jcl
                        sy.dma_start(
                            out=wcon_buf[:, jcl, :].rearrange("p (r d) -> p r d", d=128),
                            in_=wdn_all.rearrange("(r c p) d -> c p r d",
                                                  r=8, p=128)[jc]).then_inc(s_wc, 16)
                    if hf == 1:
                        for tl in range(10):
                            if tl < 8:
                                tt = th * 8 + tl
                                if nxr >= 2:
                                    sy.wait_ge(s_asm, nxr - 1)
                                sy.dma_start(out=x_re[:, tt % 2, :],
                                             in_=x_own[tt * 128:(tt + 1) * 128, :]
                                             ).then_inc(s_xP[tt % 2], 16)
                                nxr += 1
                            if tl >= 2:
                                ts_ = th * 8 + tl - 2
                                sy.wait_ge(s_asm, ts_ + 1)
                                sy.dma_start(out=out_own[ts_ * 128:(ts_ + 1) * 128, :],
                                             in_=outb[:, ts_ % 2, :]).then_inc(s_stP[ts_ % 2], 16)
            sy.wait_ge(s_stA, 16 * 8)
            sy.wait_ge(s_stB, 16 * 8)

        @sect(6, block.tensor)
        def _(te):
            te.wait_ge(s_ev, NBLK)
            for th in range(2):
                for hf in range(2):
                    u = th * 2 + hf
                    for jtl in range(16):
                        gjt = u * 16 + jtl
                        gw = u * 16 + jtl
                        te.wait_ge(s_wsP[gw % 2], 16 * (gw // 2 + 1))
                        if gjt >= 2:
                            te.wait_ge(s_gel, gjt - 1)
                        hp = A(0, 1024) if gjt % 2 == 0 else A(1024, 2048)
                        for dc in range(DC):
                            for hh in range(2):
                                ins = te.matmul(
                                    hp[:, hh * 512:(hh + 1) * 512],
                                    wup_buf[:, gjt % 2, dc, :],
                                    xn_fm[:, dc, th * 1024 + hh * 512:
                                          th * 1024 + (hh + 1) * 512],
                                    start=(dc == 0), stop=(dc == DC - 1))
                        ins.then_inc(s_hps, 1)
                    te.wait_ge(s_wc, 32 + 16 * (u + 1) * 16)
                    te.wait_ge(s_gel, u * 16 + 16)
                    for tl in range(8):
                        gt = u * 8 + tl
                        if gt >= 2:
                            te.wait_ge(s_ycon, gt - 1)
                        yp = A(2048, 3072) if gt % 2 == 0 else A(3072, 4096)
                        for jcl in range(16):
                            for hh in range(2):
                                ins = te.matmul(
                                    yp[:, hh * 512:(hh + 1) * 512],
                                    hg[:, jcl, tl * 128:(tl + 1) * 128],
                                    wcon_buf[:, jcl, hh * 512:(hh + 1) * 512],
                                    start=(jcl == 0),
                                    stop=(hf == 0 and jcl == 15))
                        if hf == 1:
                            for hh in range(2):
                                ins = te.matmul(yp[:, hh * 512:(hh + 1) * 512],
                                                ones_f1[:, :],
                                                bcon_row[:, hh * 512:(hh + 1) * 512],
                                                start=False, stop=False)
                                ins = te.matmul(yp[:, hh * 512:(hh + 1) * 512],
                                                ones_f1[:, :],
                                                dnb_row[:, hh * 512:(hh + 1) * 512],
                                                start=False, stop=True)
                        ins.then_inc(s_yps, 1)

        @sect(6, block.scalar)
        def _(s):
            s.wait_ge(s_pool, 3)
            for th in range(2):
                for hf in range(2):
                    u = th * 2 + hf
                    if u >= 1:
                        s.wait_ge(s_yps, u * 8)
                    for jtl in range(16):
                        jt = hf * 16 + jtl
                        gjt = u * 16 + jtl
                        s.wait_ge(s_hps, gjt + 1)
                        hp = A(0, 1024) if gjt % 2 == 0 else A(1024, 2048)
                        ins = s.activation(out=hg[:, jtl, :], in_=hp[:, :],
                                           func=AF.Gelu, bias=bmid_fm[:, jt:jt + 1],
                                           scale=1.0)
                        ins.then_inc(s_gel, 1)
                    if hf == 0:
                        for tl in range(8):
                            gt = u * 8 + tl
                            s.wait_ge(s_yps, gt + 1)
                            yp = A(2048, 3072) if gt % 2 == 0 else A(3072, 4096)
                            ins = s.copy(y1[:, tl, :], yp[:, :])
                            ins.then_inc(s_ycon, 1)

        @sect(6, block.vector)
        def _(v):
            for th in range(2):
                u = th * 2 + 1
                for tl in range(8):
                    tt = th * 8 + tl
                    gt = u * 8 + tl
                    v.wait_ge(s_yps, gt + 1)
                    v.wait_ge(s_xP[tt % 2], 16 * (16 + tt // 2 + 1))
                    if tt >= 2:
                        v.wait_ge(s_stP[tt % 2], 16 * (tt // 2))
                    yp = A(2048, 3072) if gt % 2 == 0 else A(3072, 4096)
                    ins = v.scalar_tensor_tensor(out=t1_sb[:, tt % 2, :], in0=yp[:, :],
                                                 scalar=1.0, in1=y1[:, tl, :],
                                                 op0=ALU.mult, op1=ALU.add)
                    ins.then_inc(s_ycon, 1)
                    v.drain()
                    v.tensor_add(t1_sb[:, tt % 2, :], t1_sb[:, tt % 2, :],
                                 attn_sb[:, tt, :])
                    v.drain()
                    ins = v.tensor_add(outb[:, tt % 2, :], t1_sb[:, tt % 2, :],
                                       x_re[:, tt % 2, :])
                    ins.then_inc(s_asm, 1)
                    v.drain()
        if DEBUG:
            sc_att.close()
            sc_pp.close()
            sc_qg.close()
    return nc


_NC_CACHE = {}


def _get_nc():
    if "nc" not in _NC_CACHE:
        _NC_CACHE["nc"] = build_program()
    return _NC_CACHE["nc"]


def _stage_vectors(weights, dim):
    ns = weights.shape[0]
    out = np.zeros((ns, 2, dim), np.float32)
    j = np.arange(dim)
    for s in range(ns):
        sig = 1 << s
        w = np.asarray(weights[s], np.float32)
        pair = (j >> (s + 1)) * sig + (j & (sig - 1))
        bit = (j & sig) != 0
        out[s, 0] = np.where(bit, w[pair, 1, 1], w[pair, 0, 0])
        out[s, 1] = np.where(bit, w[pair, 1, 0], w[pair, 0, 1])
    return out


def kernel(L0_attn_x, L0_attn_W_qk, L0_attn_W_v, L0_attn_alibi_slope,
           L0_ffn_W_exp, L0_ffn_b_exp, L0_ffn_up_weights, L0_ffn_up_bias,
           L0_ffn_down_weights, L0_ffn_W_con, L0_ffn_b_con, L0_ffn_down_bias):
    x = np.ascontiguousarray(np.asarray(L0_attn_x, np.float32))
    upv = _stage_vectors(np.asarray(L0_ffn_up_weights), DH)
    dnv = _stage_vectors(np.asarray(L0_ffn_down_weights), DH)

    def fm_slices(vec):
        out = np.zeros((128, 5 * 2 * JCN), np.float32)
        for s in range(7, 12):
            for t in range(2):
                out[:, ((s - 7) * 2 + t) * JCN:((s - 7) * 2 + t + 1) * JCN] = \
                    vec[s, t].reshape(JCN, 128).T
        return np.ascontiguousarray(out)

    upvec_fm = fm_slices(upv)
    dnvec_fm = fm_slices(dnv)
    wqk_c = np.ascontiguousarray(np.asarray(L0_attn_W_qk, np.float32))
    wv_c = np.ascontiguousarray(np.asarray(L0_attn_W_v, np.float32))
    al_c = np.ascontiguousarray(np.asarray(L0_attn_alibi_slope, np.float32))
    wexp_c = np.asarray(L0_ffn_W_exp, np.float32)
    wcon_c = np.asarray(L0_ffn_W_con, np.float32)
    bexp_c = np.ascontiguousarray(np.asarray(L0_ffn_b_exp, np.float32))
    upb_c = np.ascontiguousarray(np.asarray(L0_ffn_up_bias, np.float32))
    bcon_c = np.ascontiguousarray(np.asarray(L0_ffn_b_con, np.float32))
    dnb_c = np.ascontiguousarray(np.asarray(L0_ffn_down_bias, np.float32))

    in_maps = []
    for core in range(8):
        b, j = core // 4, core % 4
        rows = np.concatenate([np.arange(POS(j, i), POS(j, i) + 256)
                               for i in range(NBLK)])
        in_maps.append({
            "x_own": np.ascontiguousarray(x[b][rows]),
            "wqk": wqk_c, "wv": wv_c, "alibi": al_c,
            "wexp_own": np.ascontiguousarray(wexp_c[j * 128:(j + 1) * 128, :]),
            "wcon_own": np.ascontiguousarray(wcon_c[:, j * 128:(j + 1) * 128]),
            "b_exp": bexp_c, "up_bias": upb_c, "b_con": bcon_c, "dn_bias": dnb_c,
            "qpos": np.ascontiguousarray(rows.astype(np.float32)),
            "upvec_sl": np.ascontiguousarray(upv[0:7, :, j * 512:(j + 1) * 512]),
            "dnvec_sl": np.ascontiguousarray(dnv[0:7, :, j * 512:(j + 1) * 512]),
            "upvec_fm": upvec_fm, "dnvec_fm": dnvec_fm,
        })

    nc = _get_nc()
    res = run_bass_kernel_spmd(nc, in_maps, core_ids=list(range(8)))
    out = np.zeros((B, S, D), np.float32)
    for core in range(8):
        b, j = core // 4, core % 4
        oo = res.results[core]["out_own"]
        for i in range(NBLK):
            p = POS(j, i)
            out[b, p:p + 256] = oo[i * 256:(i + 1) * 256]
    return out


# version 04 (final)
# speedup vs baseline: 1.1766x; 1.1578x over previous
import numpy as np
from contextlib import ExitStack

import concourse.bass as bass
import concourse.mybir as mybir
from concourse.bass_utils import run_bass_kernel_spmd

F32 = mybir.dt.float32
BF16 = mybir.dt.bfloat16
I32 = mybir.dt.int32
AF = mybir.ActivationFunctionType
ALU = mybir.AluOpType
AX = mybir.AxisListType

B = 2
S = 8192
D = 1024
DH = 4096
DC = 8
JCN = 32
NBLK = 8
NT = 16
SCALE = 1.0 / 32.0
PEN = -3.2e10

DEBUG = False
import os
PH = int(os.environ.get('BASS_KERNEL_PHASE', '6'))
SUB = int(os.environ.get('BASS_KERNEL_SUB', '9'))


def POS(j, i):
    return (4 * i + (i + j) % 4) * 256


def build_program():
    nc = bass.Bass("TRN2", target_bir_lowering=False, debug=False)

    def din(name, shape, dtype=F32):
        return nc.dram_tensor(name, shape, dtype, kind="ExternalInput").ap()

    x_own = din("x_own", [2048, D])
    wqk = din("wqk", [D, D])
    wv = din("wv", [D, D])
    alibi = din("alibi", [1])
    wexp_own = din("wexp_own", [128, DH])
    wcon_own = din("wcon_own", [DH, 128])
    b_exp_in = din("b_exp", [DH])
    up_bias_in = din("up_bias", [DH])
    b_con_in = din("b_con", [D])
    dn_bias_in = din("dn_bias", [D])
    qpos_in = din("qpos", [2048])
    upvec_sl = din("upvec_sl", [7, 2, 512])
    dnvec_sl = din("dnvec_sl", [7, 2, 512])
    upvec_fm_in = din("upvec_fm", [128, 5 * 2 * JCN])
    dnvec_fm_in = din("dnvec_fm", [128, 5 * 2 * JCN])

    out_own = nc.dram_tensor("out_own", [2048, D], F32, kind="ExternalOutput").ap()
    dbg = {}
    if DEBUG:
        for nm, shp in [("xnfm", [128, DC * 2048]), ("qfm", [128, DC * 2048]),
                        ("extra", [1, 2048]), ("wupc", [128, DH]),
                        ("wdnc", [128, JCN * 128]), ("bmid", [128, JCN]),
                        ("attn", [128, NT * D]), ("xsb", [128, DH]),
                        ("esb", [128, DH])]:
            dbg[nm] = nc.dram_tensor("dbg_" + nm, shp, F32, kind="ExternalOutput").ap()

    ccin_xn = nc.dram_tensor("ccin_xn", [DC * 128, 2048], BF16, kind="Internal").ap()
    xn_all = nc.dram_tensor("xn_all", [4 * DC * 128, 2048], BF16, kind="Internal").ap()
    ccin_v = nc.dram_tensor("ccin_v", [2048, D], BF16, kind="Internal").ap()
    v_all = nc.dram_tensor("v_all", [4 * 2048, D], BF16, kind="Internal").ap()
    ccin_xe = nc.dram_tensor("ccin_xe", [256, 512], BF16, kind="Internal").ap()
    xe_all = nc.dram_tensor("xe_all", [8 * 256, 512], BF16, kind="Internal").ap()
    ccin_wup = nc.dram_tensor("ccin_wup", [128, DH], BF16, kind="Internal").ap()
    wup_all = nc.dram_tensor("wup_all", [8 * 128, DH], BF16, kind="Internal").ap()
    ccin_wdn = nc.dram_tensor("ccin_wdn", [DH, 128], BF16, kind="Internal").ap()
    wdn_all = nc.dram_tensor("wdn_all", [8 * DH, 128], BF16, kind="Internal").ap()

    g4 = [[0, 1, 2, 3], [4, 5, 6, 7]]
    g8 = [list(range(8))]

    main = ExitStack()
    e = main.enter_context
    with main:
        xn_fm = e(nc.sbuf_tensor("xn_fm", [128, DC, 2048], BF16))
        attn_sb = e(nc.sbuf_tensor("attn_sb", [128, NT, D], BF16))
        kpos_f = e(nc.sbuf_tensor("kpos_f", [128, 64], F32))
        alibi_k = e(nc.sbuf_tensor("alibi_k", [128, 64], F32))
        slope_c = e(nc.sbuf_tensor("slope_c", [128, 1], F32))
        ones_f1 = e(nc.sbuf_tensor("ones_f1", [1, 128], F32))
        ones_bc = e(nc.sbuf_tensor("ones_bc", [128, 1], BF16))
        ident_bf = e(nc.sbuf_tensor("ident_bf", [128, 128], BF16))
        qpos_bc = e(nc.sbuf_tensor("qpos_bc", [128, NBLK, 256], F32))
        bmid_fm = e(nc.sbuf_tensor("bmid_fm", [128, JCN], F32))
        upv_fm = e(nc.sbuf_tensor("upv_fm", [128, 5 * 2 * JCN], BF16))
        dnv_fm = e(nc.sbuf_tensor("dnv_fm", [128, 5 * 2 * JCN], BF16))
        iotac_f = e(nc.sbuf_tensor("iotac_f", [128, 1], F32))
        iota4_f = e(nc.sbuf_tensor("iota4_f", [128, 512], BF16))

        arena = e(nc.psum_tensor("arena", [128, 4096], F32))

        def A(c0, c1, p0=0, p1=128):
            return arena[p0:p1, c0:c1]
        tpsum = arena[:, 3584:4096].bitcast(BF16)
        tp6 = arena[:, 3072:3584].bitcast(BF16)
        tb0 = arena[:, 0:512].bitcast(BF16)
        tb1 = arena[:, 512:1024].bitcast(BF16)

        sem_names = ("s_ld", "s_x", "s_kv", "s_ws", "s_cc", "s_st", "s_scp",
                     "s_avp", "s_exp", "s_msk", "s_ev", "s_rz", "s_pe", "s_act",
                     "s_dve", "s_pool", "s_hps", "s_gel", "s_yps", "s_ycon",
                     "s_asm", "s_pp_pe", "s_pp_dve", "s_pp_pool", "s_xe", "s_vc", "s_init",
                     "s_xA", "s_xB", "s_kvA", "s_kvB", "s_wsA", "s_wsB",
                     "s_wc", "s_vcA", "s_vcB", "s_stA", "s_stB", "s_xedone", "s_kvC")
        SM = {nm: e(nc.semaphore(nm)) for nm in sem_names}
        (s_ld, s_x, s_kv, s_ws, s_cc, s_st, s_scp, s_avp, s_exp, s_msk, s_ev,
         s_rz, s_pe, s_act, s_dve, s_pool, s_hps, s_gel, s_yps, s_ycon, s_asm,
         s_pp_pe, s_pp_dve, s_pp_pool, s_xe, s_vc, s_init, s_xA, s_xB, s_kvA,
         s_kvB, s_wsA, s_wsB, s_wc, s_vcA, s_vcB, s_stA, s_stB, s_xedone,
         s_kvC) = (SM[n] for n in sem_names)
        s_xP = [s_xA, s_xB]
        s_kvP = [s_kvA, s_kvB, s_kvC]
        s_wsP = [s_wsA, s_wsB]
        s_vcP = [s_vcA, s_vcB]
        s_stP = [s_stA, s_stB]

        block = e(nc.Block())

        def sect(ph, engdec):
            if PH >= ph:
                return engdec
            return lambda f: None
        n_ld = [0]

        def colbc(t, coloff, n):
            row = int(np.prod(t.shape[1:]))
            return bass.AP(t, coloff, [[row, 128], [0, n]])


        def pldma(g, out, in_):
            g.dma_start(out=out, in_=in_).then_inc(s_ld, 16)
            n_ld[0] += 16

        sc_qg = ExitStack()
        q_fm = sc_qg.enter_context(nc.sbuf_tensor("q_fm", [128, DC, 2048], BF16))
        extra_row = sc_qg.enter_context(nc.sbuf_tensor("extra_row", [1, 2048], F32))
        wexpT = sc_qg.enter_context(nc.sbuf_tensor("wexpT", [128, JCN, 128], BF16))
        bexp_fm = sc_qg.enter_context(nc.sbuf_tensor("bexp_fm", [128, JCN], BF16))
        upb_fm = sc_qg.enter_context(nc.sbuf_tensor("upb_fm", [128, JCN], F32))

        sc_ln = ExitStack()
        iota_i = sc_ln.enter_context(nc.sbuf_tensor("iota_i", [128, 512], I32))
        x_tile = sc_ln.enter_context(nc.sbuf_tensor("x_tile", [128, 2, D], F32))
        ln_sum = sc_ln.enter_context(nc.sbuf_tensor("ln_sum", [128, NT], F32))
        ln_sq = sc_ln.enter_context(nc.sbuf_tensor("ln_sq", [128, NT], F32))
        rstd = sc_ln.enter_context(nc.sbuf_tensor("rstd", [128, NT], F32))
        nmr = sc_ln.enter_context(nc.sbuf_tensor("nmr", [128, NT], F32))
        var_t = sc_ln.enter_context(nc.sbuf_tensor("var_t", [128, NT], F32))
        std_t = sc_ln.enter_context(nc.sbuf_tensor("std_t", [128, NT], F32))
        m2_t = sc_ln.enter_context(nc.sbuf_tensor("m2_t", [128, NT], F32))
        xn_tm = sc_ln.enter_context(nc.sbuf_tensor("xn_tm", [128, 2, D], BF16))

        sc_wv = ExitStack()
        wv_bf = sc_wv.enter_context(nc.sbuf_tensor("wv_bf", [128, DC, D], BF16))

        sc_vp = ExitStack()
        wvT = sc_vp.enter_context(nc.sbuf_tensor("wvT", [128, DC, D], BF16))
        v_ping = sc_vp.enter_context(nc.sbuf_tensor("v_ping", [128, 2, D], BF16))

        sc_qp = ExitStack()
        wqk_bf = sc_qp.enter_context(nc.sbuf_tensor("wqk_bf", [128, DC, D], BF16))
        q2scr = sc_qp.enter_context(nc.sbuf_tensor("q2scr", [128, 2, 512], BF16))
        rs_c = sc_qp.enter_context(nc.sbuf_tensor("rs_c", [1, 1], F32))

        sc_bx = ExitStack()
        bexp_32 = sc_bx.enter_context(nc.sbuf_tensor("bexp_32", [32, 128], BF16))
        upb_32 = sc_bx.enter_context(nc.sbuf_tensor("upb_32", [32, 128], BF16))
        wexp_in = sc_bx.enter_context(nc.sbuf_tensor("wexp_in", [128, DH], BF16))

        @sect(1, block.gpsimd)
        def _(g):
            g.iota(iota_i[:, 128:129], [[0, 1]], channel_multiplier=1)
            g.iota(iota_i[:, 192:256], [[128, 64]], channel_multiplier=1)
            g.drain()
            g.tensor_copy(iotac_f[:, :], iota_i[:, 128:129])
            g.tensor_copy(kpos_f[:, :], iota_i[:, 192:256])
            g.drain()
            g.iota(iota_i[:, 0:512].rearrange("p (a b) -> p a b", b=128),
                   [[0, 4], [1, 128]], channel_multiplier=0)
            g.drain()
            g.tensor_copy(iota4_f[:, :], iota_i[:, 0:512])
            g.memset(ones_f1[:, :], 1.0)
            g.memset(ones_bc[:, :], 1.0)
            g.drain()
            g.engine_nop().then_inc(s_init, 1)
            pldma(g, wqk_bf[:, :, :], wqk.rearrange("(dc p) e -> p dc e", p=128))
            pldma(g, wv_bf[:, :, :], wv.rearrange("(dc p) e -> p dc e", p=128))
            pldma(g, bexp_32[:, :], b_exp_in.rearrange("(a b) -> a b", b=128))
            pldma(g, upb_32[:, :], up_bias_in.rearrange("(a b) -> a b", b=128))
            pldma(g, upv_fm[:, :], upvec_fm_in[:, :])
            pldma(g, dnv_fm[:, :], dnvec_fm_in[:, :])
            pldma(g, slope_c[:, :], bass.AP(alibi.tensor, 0, [[0, 128], [1, 1]]))
            pldma(g, qpos_bc[:, :, :],
                  bass.AP(qpos_in.tensor, 0, [[0, 128], [256, NBLK], [1, 256]]))
            pldma(g, wexp_in[:, :], wexp_own[:, :])
            g.wait_ge(s_ld, n_ld[0])
            g.tensor_tensor(out=alibi_k[:, :], in0=kpos_f[:, :],
                            in1=colbc(slope_c, 0, 64), op=ALU.mult)
            g.drain()
            g.engine_nop().then_inc(s_pool, 1)
        LD_EARLY = n_ld[0]

        @sect(1, block.vector)
        def _(v):
            v.wait_ge(s_init, 1)
            v.tensor_scalar(out=ident_bf[:, :], in0=iota4_f[:, 0:128],
                            scalar1=iotac_f[:, 0:1], scalar2=None, op0=ALU.is_equal)
            v.drain()
            v.tensor_scalar(out=iota4_f[:, :], in0=iota4_f[:, :],
                            scalar1=iotac_f[:, 0:1], scalar2=None, op0=ALU.is_equal)
            v.drain()
            v.engine_nop().then_inc(s_init, 1)

        @sect(1, block.sync)
        def _(sy):
            for t in range(NT):
                if t >= 2:
                    sy.wait_ge(s_dve, t - 1)
                    sy.wait_ge(s_act, t - 1)
                sy.dma_start(out=x_tile[:, t % 2, :],
                             in_=x_own[t * 128:(t + 1) * 128, :]).then_inc(s_xP[t % 2], 16)

        @sect(1, block.vector)
        def _(v):
            for t in range(NT):
                v.wait_ge(s_xP[t % 2], 16 * (t // 2 + 1))
                v.reduce_sum(ln_sum[:, t:t + 1], x_tile[:, t % 2, :],
                             axis=AX.X).then_inc(s_dve, 1)
        DVE_L1 = NT

        @sect(1, block.scalar)
        def _(s):
            for t in range(NT):
                s.wait_ge(s_xP[t % 2], 16 * (t // 2 + 1))
                s.activation(out=A(2048, 3072), in_=x_tile[:, t % 2, :], func=AF.Square,
                             accum_out=ln_sq[:, t:t + 1]).then_inc(s_act, 1)
                s.drain()
        ACT_L1 = NT

        @sect(1, block.vector)
        def _(v):
            v.wait_ge(s_dve, DVE_L1)
            v.wait_ge(s_act, ACT_L1)
            v.tensor_scalar(out=ln_sum[:, :], in0=ln_sum[:, :], scalar1=1.0 / D,
                            scalar2=None, op0=ALU.mult)
            v.drain()
            v.tensor_mul(m2_t[:, :], ln_sum[:, :], ln_sum[:, :])
            v.drain()
            v.scalar_tensor_tensor(out=var_t[:, :], in0=ln_sq[:, :], scalar=1.0 / D,
                                   in1=m2_t[:, :], op0=ALU.mult, op1=ALU.subtract)
            v.drain()
            v.tensor_scalar(out=var_t[:, :], in0=var_t[:, :], scalar1=1e-5,
                            scalar2=None, op0=ALU.add).then_inc(s_dve, 1)
        DVE_VAR = DVE_L1 + 1

        @sect(1, block.scalar)
        def _(s):
            s.wait_ge(s_dve, DVE_VAR)
            s.activation(out=std_t[:, :], in_=var_t[:, :], func=AF.Sqrt,
                         bias=0.0, scale=1.0).then_inc(s_act, 1)
        ACT_STD = ACT_L1 + 1

        @sect(1, block.vector)
        def _(v):
            v.wait_ge(s_act, ACT_STD)
            v.reciprocal(rstd[:, :], std_t[:, :])
            v.drain()
            v.tensor_mul(nmr[:, :], ln_sum[:, :], rstd[:, :])
            v.drain()
            v.tensor_scalar(out=nmr[:, :], in0=nmr[:, :], scalar1=-1.0,
                            scalar2=None, op0=ALU.mult).then_inc(s_dve, 1)
        DVE_RSTD = DVE_VAR + 1

        @sect(1, block.sync)
        def _(sy):
            for t in range(NT):
                if t < 2:
                    sy.wait_ge(s_act, ACT_L1 - 1 + t)
                    sy.wait_ge(s_dve, DVE_L1 - 1 + t)
                else:
                    sy.wait_ge(s_act, ACT_STD + t - 1)
                sy.dma_start(out=x_tile[:, t % 2, :],
                             in_=x_own[t * 128:(t + 1) * 128, :]).then_inc(s_xP[t % 2], 16)

        @sect(1, block.scalar)
        def _(s):
            s.wait_ge(s_dve, DVE_RSTD)
            for t in range(NT):
                s.wait_ge(s_xP[t % 2], 16 * (8 + t // 2 + 1))
                if t >= 2:
                    s.wait_ge(s_pe, t - 1)
                s.activation(out=xn_tm[:, t % 2, :], in_=x_tile[:, t % 2, :],
                             func=AF.Identity, bias=nmr[:, t:t + 1],
                             scale=rstd[:, t:t + 1]).then_inc(s_act, 1)
        ACT_XN = ACT_STD + NT

        @sect(1, block.tensor)
        def _(te):
            te.wait_ge(s_pool, 1)
            te.wait_ge(s_init, 2)
            for t in range(NT):
                te.wait_ge(s_act, ACT_STD + t + 1)
                if t >= 1:
                    te.wait_ge(s_dve, DVE_RSTD + t)
                for dc in range(DC):
                    ins = te.transpose(tpsum[:, dc * 128:(dc + 1) * 128],
                                       xn_tm[:, t % 2, dc * 128:(dc + 1) * 128],
                                       ident_bf[:, :])
                ins.then_inc(s_pe, 1)
        PE_XNT = NT

        @sect(1, block.vector)
        def _(v):
            for t in range(NT):
                v.wait_ge(s_pe, t + 1)
                for dc in range(DC):
                    ins = v.tensor_copy(xn_fm[:, dc, t * 128:(t + 1) * 128],
                                        tpsum[:, dc * 128:(dc + 1) * 128])
                ins.then_inc(s_dve, 1)
        DVE_XNF = DVE_RSTD + NT

        @sect(1, block.gpsimd)
        def _(g):
            g.wait_ge(s_dve, DVE_XNF)
            g.dma_start(out=ccin_xn.rearrange("(dc p) t -> p dc t", p=128),
                        in_=xn_fm[:, :, :]).then_inc(s_ld, 16)
            n_ld[0] += 16
            g.wait_ge(s_ld, n_ld[0])
            g.collective_compute("AllGather", ALU.bypass, replica_groups=g4,
                                 ins=[ccin_xn[:, :].opt()],
                                 outs=[xn_all[:, :].opt()]).then_inc(s_cc, 1)

        @sect(2, block.tensor)
        def _(te):
            te.wait_ge(s_ld, LD_EARLY)
            te.wait_ge(s_dve, DVE_XNF)
            n = 0
            for dc in range(DC):
                for ec in range(DC):
                    if n >= 2:
                        te.wait_ge(s_dve, DVE_XNF + n - 1)
                    tp = tpsum if n % 2 == 0 else tp6
                    ins = te.transpose(tp[:, 0:128],
                                       wv_bf[:, ec, dc * 128:(dc + 1) * 128],
                                       ident_bf[:, :])
                    ins.then_inc(s_pe, 1)
                    n += 1
        PE_WVT = PE_XNT + 64

        @sect(2, block.vector)
        def _(v):
            n = 0
            for dc in range(DC):
                for ec in range(DC):
                    v.wait_ge(s_pe, PE_XNT + n + 1)
                    tp = tpsum if n % 2 == 0 else tp6
                    ins = v.tensor_copy(wvT[:, dc, ec * 128:(ec + 1) * 128],
                                        tp[:, 0:128])
                    ins.then_inc(s_dve, 1)
                    n += 1
        DVE_WVT = DVE_XNF + 64

        @sect(2, block.tensor)
        def _(te):
            te.wait_ge(s_dve, DVE_WVT)
            for t in range(NT):
                if t >= 2:
                    te.wait_ge(s_act, ACT_XN + t - 1)
                ps = A(0, 1024) if t % 2 == 0 else A(1024, 2048)
                for dc in range(DC):
                    for h in range(2):
                        ins = te.matmul(ps[:, h * 512:(h + 1) * 512],
                                        xn_fm[:, dc, t * 128:(t + 1) * 128],
                                        wvT[:, dc, h * 512:(h + 1) * 512],
                                        start=(dc == 0), stop=(dc == DC - 1))
                ins.then_inc(s_pe, 1)
        PE_VP = PE_WVT + NT

        @sect(2, block.scalar)
        def _(s):
            for t in range(NT):
                s.wait_ge(s_pe, PE_WVT + t + 1)
                if t >= 2:
                    s.wait_ge(s_vcP[t % 2], 16 * (t // 2))
                ps = A(0, 1024) if t % 2 == 0 else A(1024, 2048)
                s.copy(v_ping[:, t % 2, :], ps[:, :]).then_inc(s_act, 1)
        ACT_VP = ACT_XN + NT

        @sect(2, block.sync)
        def _(sy):
            for t in range(NT):
                sy.wait_ge(s_act, ACT_XN + t + 1)
                sy.dma_start(out=ccin_v[t * 128:(t + 1) * 128, :],
                             in_=v_ping[:, t % 2, :]).then_inc(s_vcP[t % 2], 16)

        @sect(2, block.gpsimd)
        def _(g):
            g.wait_ge(s_vcA, 16 * (NT // 2))
            g.wait_ge(s_vcB, 16 * (NT // 2))
            g.collective_compute("AllGather", ALU.bypass, replica_groups=g4,
                                 ins=[ccin_v[:, :].opt()],
                                 outs=[v_all[:, :].opt()]).then_inc(s_cc, 1)

        @sect(2, block.tensor)
        def _(te):
            te.wait_ge(s_act, ACT_VP)
            n = 0
            for et in range(DC):
                for tb in range(4):
                    if n >= 2:
                        te.wait_ge(s_act, ACT_VP + n - 1)
                    ps = A(0, 512) if n % 2 == 0 else A(512, 1024)
                    for dc in range(DC):
                        ins = te.matmul(ps[:, :],
                                        wqk_bf[:, dc, et * 128:(et + 1) * 128],
                                        xn_fm[:, dc, tb * 512:(tb + 1) * 512],
                                        start=(dc == 0), stop=(dc == DC - 1))
                    ins.then_inc(s_pe, 1)
                    n += 1
        PE_QP = PE_VP + 32

        @sect(2, block.scalar)
        def _(s):
            n = 0
            for et in range(DC):
                for tb in range(4):
                    s.wait_ge(s_pe, PE_VP + n + 1)
                    ps = A(0, 512) if n % 2 == 0 else A(512, 1024)
                    s.copy(q_fm[:, et, tb * 512:(tb + 1) * 512], ps[:, :]).then_inc(s_act, 1)
                    n += 1
        ACT_QP = ACT_VP + 32

        @sect(2, block.scalar)
        def _(s):
            n = 0
            for tb in range(4):
                for et in range(DC):
                    s.wait_ge(s_act, ACT_VP + et * 4 + tb + 1)
                    if n >= 2:
                        s.wait_ge(s_pe, PE_QP + n - 1)
                    s.activation(out=q2scr[:, n % 2, :],
                                 in_=q_fm[:, et, tb * 512:(tb + 1) * 512],
                                 func=AF.Square).then_inc(s_act, 1)
                    n += 1
        ACT_Q2 = ACT_QP + 32

        @sect(2, block.tensor)
        def _(te):
            n = 0
            for tb in range(4):
                for et in range(DC):
                    te.wait_ge(s_act, ACT_QP + n + 1)
                    ins = te.matmul(A(1024 + tb * 512, 1024 + (tb + 1) * 512, 0, 1),
                                    ones_bc[:, :], q2scr[:, n % 2, :],
                                    start=(et == 0), stop=(et == DC - 1))
                    ins.then_inc(s_pe, 1)
                    n += 1
        PE_Q2 = PE_QP + 32

        @sect(2, block.scalar)
        def _(s):
            s.wait_ge(s_pe, PE_Q2)
            s.activation(out=extra_row[:, :], in_=A(1024, 3072, 0, 1), func=AF.Sqrt,
                         bias=0.0, scale=1.0).then_inc(s_act, 1)
            s.activation(out=rs_c[:, :], in_=slope_c[0:1, 0:1], func=AF.Relu,
                         bias=0.0, scale=1.0).then_inc(s_act, 1)
        ACT_BND = ACT_Q2 + 2

        @sect(2, block.vector)
        def _(v):
            v.wait_ge(s_act, ACT_BND)
            v.tensor_scalar(out=extra_row[:, :], in0=extra_row[:, :], scalar1=1.01,
                            scalar2=1.0, op0=ALU.mult, op1=ALU.add)
            v.drain()
            v.scalar_tensor_tensor(out=extra_row[:, :],
                                   in0=qpos_bc[0:1, :, :].rearrange("p a b -> p (a b)"),
                                   scalar=rs_c[0:1, 0:1], in1=extra_row[:, :],
                                   op0=ALU.mult, op1=ALU.add)
            v.drain()
            v.tensor_scalar(out=extra_row[:, :], in0=extra_row[:, :],
                            scalar1=-1.0 / SCALE, scalar2=None,
                            op0=ALU.mult).then_inc(s_dve, 1)
        DVE_EXT = DVE_WVT + 1

        @sect(3 if SUB >= 1 else 99, block.tensor)
        def _(te):
            te.wait_ge(s_ld, LD_EARLY)
            for c in range(JCN):
                if c >= 2:
                    te.wait_ge(s_pp_dve, c - 1)
                tp = tpsum if c % 2 == 0 else tp6
                ins = te.transpose(tp[:, 0:128],
                                   wexp_in[:, c * 128:(c + 1) * 128], ident_bf[:, :])
                ins.then_inc(s_pp_pe, 1)
            te.wait_ge(s_pp_dve, JCN - 1)
            ins = te.transpose(tpsum[:, 0:32], bexp_32[:, :], ident_bf[0:32, 0:32])
            ins.then_inc(s_pp_pe, 1)
            ins = te.transpose(tpsum[:, 32:64], upb_32[:, :], ident_bf[0:32, 0:32])
            ins.then_inc(s_pp_pe, 1)
        PP_PE_T0 = JCN + 2

        @sect(3 if SUB >= 1 else 99, block.vector)
        def _(v):
            for c in range(JCN):
                v.wait_ge(s_pp_pe, c + 1)
                tp = tpsum if c % 2 == 0 else tp6
                ins = v.tensor_copy(wexpT[:, c, :], tp[:, 0:128])
                ins.then_inc(s_pp_dve, 1)
            v.wait_ge(s_pp_pe, JCN + 2)
            ins = v.tensor_copy(bexp_fm[:, :], tpsum[:, 0:32])
            ins.then_inc(s_pp_dve, 1)
            ins = v.tensor_copy(upb_fm[:, :], tpsum[:, 32:64])
            ins.then_inc(s_pp_dve, 1)
        PP_DVE_T0 = JCN + 2

        sc_bx.close()
        sc_qp.close()
        sc_vp.close()
        sc_wv.close()
        sc_ln.close()

        sc_pp = ExitStack()
        app = sc_pp.enter_context
        wcon_in = app(nc.sbuf_tensor("wcon_in", [128, JCN, 128], BF16))
        z_a = app(nc.sbuf_tensor("z_a", [128, JCN, 128], BF16))
        z_t = app(nc.sbuf_tensor("z_t", [128, 8, 128], BF16))
        zd_a = app(nc.sbuf_tensor("zd_a", [128, JCN, 128], BF16))
        b1_fm = app(nc.sbuf_tensor("b1_fm", [128, JCN], F32))
        b2_fm = app(nc.sbuf_tensor("b2_fm", [128, JCN], F32))
        bt_f = app(nc.sbuf_tensor("bt_f", [128, JCN], F32))
        X_sb = app(nc.sbuf_tensor("X_sb", [128, 8, 512], BF16))
        E_sb = app(nc.sbuf_tensor("E_sb", [128, 8, 512], BF16))
        wupc = app(nc.sbuf_tensor("wupc", [128, DH], BF16))
        wdnc = app(nc.sbuf_tensor("wdnc", [128, JCN, 128], BF16))

        sc_xe = ExitStack()
        xe_p = sc_xe.enter_context(nc.sbuf_tensor("xe_p", [128, 2, 512], F32))
        xe_m = sc_xe.enter_context(nc.sbuf_tensor("xe_m", [128, 4, 256], F32))
        xe_w = sc_xe.enter_context(nc.sbuf_tensor("xe_w", [128, 2, 512], F32))
        xe_bf = sc_xe.enter_context(nc.sbuf_tensor("xe_bf", [128, 512], BF16))

        up_seq = [z_a, wexpT, z_a, wexpT, z_a, wexpT]
        dn_seq = [wcon_in, zd_a, wcon_in, zd_a, wcon_in, zd_a]
        Z2_FINAL = up_seq[5]
        ZD_FINAL = dn_seq[5]

        @sect(3, block.gpsimd)
        def _(g):
            g.wait_ge(s_pe, PE_Q2)
            g.wait_ge(s_pp_pe, PP_PE_T0)
            pldma(g, wcon_in[:, :, :], wcon_own.rearrange("(c p) d -> p c d", p=128))
            g.engine_nop().then_inc(s_pool, 1)
        LD_WEXP = n_ld[0]

        def stage_cols(g, vec_sl, sg, transposed, cur):
            sig = 1 << sg
            pldma(g, xe_w[:, 0, :], bass.AP(vec_sl.tensor, (sg * 2 + 0) * 512,
                                            [[0, 128], [1, 512]]))
            pldma(g, xe_w[:, 1, :], bass.AP(vec_sl.tensor, (sg * 2 + 1) * 512,
                                            [[0, 128], [1, 512]]))
            g.wait_ge(s_ld, n_ld[0])
            nb = 512 // (2 * sig)

            def vw(apobj, off):
                return bass.AP(apobj.tensor, apobj.offset + off,
                               [list(apobj.ap[0]), [2 * sig, nb], [1, sig]])

            def mw(k):
                return bass.AP(xe_m, k * 256, [[1024, 128], [sig, nb], [1, sig]])
            src = xe_p[:, cur, :]
            dst = xe_p[:, 1 - cur, :]
            a_in, b_in = vw(src, 0), vw(src, sig)
            dg_a, dg_b = vw(xe_w[:, 0, :], 0), vw(xe_w[:, 0, :], sig)
            of_a, of_b = vw(xe_w[:, 1, :], 0), vw(xe_w[:, 1, :], sig)
            na, nb_ = vw(dst, 0), vw(dst, sig)
            g.tensor_mul(mw(0), a_in, dg_a)
            g.tensor_mul(mw(1), b_in, of_b if transposed else of_a)
            g.tensor_mul(mw(2), b_in, dg_b)
            g.tensor_mul(mw(3), a_in, of_a if transposed else of_b)
            g.drain()
            g.tensor_add(na, mw(0), mw(1))
            g.tensor_add(nb_, mw(2), mw(3))
            g.drain()

        @sect(3 if SUB >= 2 else 99, block.gpsimd)
        def _(g):
            g.wait_ge(s_pool, 1)
            g.wait_ge(s_init, 2)
            g.wait_ge(s_pe, PE_Q2)
            g.wait_ge(s_pp_pe, PP_PE_T0)
            g.wait_ge(s_pp_dve, PP_DVE_T0)
            for half, (vec_sl, stages, tr) in enumerate(
                    [(upvec_sl, list(range(7)), False),
                     (dnvec_sl, list(range(6, -1, -1)), True)]):
                g.tensor_copy(xe_p[:, 0, :], iota4_f[:, :])
                g.drain()
                cur = 0
                for sg in stages:
                    stage_cols(g, vec_sl, sg, tr, cur)
                    cur = 1 - cur
                g.tensor_copy(xe_bf[:, :], xe_p[:, cur, :])
                g.drain()
                g.dma_start(out=ccin_xe[half * 128:(half + 1) * 128, :],
                            in_=xe_bf[:, :]).then_inc(s_ld, 16)
                n_ld[0] += 16
            g.wait_ge(s_ld, n_ld[0])
            g.engine_nop().then_inc(s_xedone, 1)
            g.collective_compute("AllGather", ALU.bypass, replica_groups=g8,
                                 ins=[ccin_xe[:, :].opt()],
                                 outs=[xe_all[:, :].opt()]).then_inc(s_cc, 1)
            g.wait_ge(s_cc, 3)
            g.wait_ge(s_pp_pe, JCN + 2)
            g.dma_start(out=X_sb[:, :, :],
                        in_=xe_all.rearrange("(r h p) c -> h p r c", r=8, p=128)[0]
                        ).then_inc(s_xe, 16)
            g.dma_start(out=E_sb[:, :, :],
                        in_=xe_all.rearrange("(r h p) c -> h p r c", r=8, p=128)[1]
                        ).then_inc(s_xe, 16)

        def hi_stage(g, s_idx, src, dst, vecs_fm, transposed):
            sc = 1 << (s_idx - 7)

            def cvec(t, c):
                return colbc(vecs_fm, ((s_idx - 7) * 2 + t) * JCN + c, 128)
            pairs = [(c, c ^ sc) for c in range(JCN) if not (c & sc)]
            for q in range(8):
                sub = pairs[q * 2:(q + 1) * 2]
                for idx, (ca, cb) in enumerate(sub):
                    o_a = cvec(1, cb) if transposed else cvec(1, ca)
                    o_b = cvec(1, ca) if transposed else cvec(1, cb)
                    g.tensor_mul(z_t[:, 4 * idx + 0, :], src[:, ca, :], cvec(0, ca))
                    g.tensor_mul(z_t[:, 4 * idx + 1, :], src[:, cb, :], o_a)
                    g.tensor_mul(z_t[:, 4 * idx + 2, :], src[:, cb, :], cvec(0, cb))
                    g.tensor_mul(z_t[:, 4 * idx + 3, :], src[:, ca, :], o_b)
                g.drain()
                for idx, (ca, cb) in enumerate(sub):
                    g.tensor_add(dst[:, ca, :], z_t[:, 4 * idx + 0, :],
                                 z_t[:, 4 * idx + 1, :])
                    g.tensor_add(dst[:, cb, :], z_t[:, 4 * idx + 2, :],
                                 z_t[:, 4 * idx + 3, :])
                g.drain()

        @sect(3 if SUB >= 3 else 99, block.gpsimd)
        def _(g):
            for k, s_idx in enumerate([11, 10, 9, 8, 7]):
                hi_stage(g, s_idx, dn_seq[k], dn_seq[k + 1], dnv_fm, transposed=True)
            g.engine_nop().then_inc(s_pp_pool, 1)

        sc_xe.close()
        sc_att = ExitStack()
        kv_k = sc_att.enter_context(nc.sbuf_tensor("kv_k", [128, 3, DC, 256], BF16))
        kv_v = sc_att.enter_context(nc.sbuf_tensor("kv_v", [128, 3, 2, D], BF16))
        pt_sb = sc_att.enter_context(nc.sbuf_tensor("pt_sb", [128, 2, 256], BF16))
        pen_sb = sc_att.enter_context(nc.sbuf_tensor("pen_sb", [128, 256], BF16))
        rz_sb = sc_att.enter_context(nc.sbuf_tensor("rz_sb", [128, 2], F32))

        seg_end_kt = []
        _g = 0
        for i in range(NBLK):
            for sg in range(4 * (i + 1)):
                seg_end_kt.append(_g + (sg + 1) * 2)
            _g += 8 * (i + 1)
        TOT_KT = _g

        def av_slice(sub):
            return A(0, 1024) if sub == 0 else A(1024, 2048)

        @sect(4, block.sync)
        def _(sy):
            sy.wait_ge(s_cc, 2)
            sy.wait_ge(s_pe, PE_Q2)
            sy.wait_ge(s_pp_pe, PP_PE_T0)
            sy.wait_ge(s_pp_dve, PP_DVE_T0)
            sy.wait_ge(s_xedone, 1)
            g_seg = 0
            for i in range(NBLK):
                for sg in range(4 * (i + 1)):
                    if g_seg >= 3:
                        sy.wait_ge(s_avp, seg_end_kt[g_seg - 3])
                    par = g_seg % 3
                    i_star, sig = sg // 4, sg % 4
                    r, o = (sig - i_star) % 4, i_star * 256
                    sy.dma_start(
                        out=kv_k[:, par, :, :],
                        in_=xn_all.rearrange("(r dc p) t -> r p dc t", r=4, dc=8)
                        [r, :, :, o:o + 256]).then_inc(s_kvP[par], 16)
                    sy.dma_start(
                        out=kv_v[:, par, :, :],
                        in_=v_all.rearrange("(r kt p) e -> r p kt e", r=4, p=128)
                        [r, :, o // 128:o // 128 + 2, :]).then_inc(s_kvP[par], 16)
                    g_seg += 1

        @sect(4, block.tensor)
        def _(te):
            te.wait_ge(s_dve, DVE_EXT)
            te.wait_ge(s_act, ACT_QP)
            n_kt = 0
            g_seg = 0
            for i in range(NBLK):
                if i >= 1:
                    te.wait_ge(s_ev, i)
                if i == 1:
                    te.wait_ge(s_pp_dve, PP_DVE_T0 + JCN + 1)
                if i == 3:
                    te.wait_ge(s_pp_dve, PP_DVE_T0 + 2 * JCN + 1)
                if i == 5:
                    te.wait_ge(s_pp_dve, PP_DVE_T0 + 3 * JCN + 1)
                for sg in range(4 * (i + 1)):
                    par = g_seg % 3
                    te.wait_ge(s_kvP[par], 32 * (g_seg // 3 + 1))
                    for kk in range(2):
                        ktg = sg * 2 + kk
                        last = (ktg == 8 * (i + 1) - 1)
                        first = (ktg == 0)
                        sc = A(3072, 3328) if n_kt % 2 == 0 else A(3584, 3840)
                        if n_kt >= 2:
                            te.wait_ge(s_exp, n_kt - 1)
                        for dc in range(DC):
                            ins = te.matmul(sc[:, :],
                                            kv_k[:, par, dc, kk * 128:(kk + 1) * 128],
                                            q_fm[:, dc, i * 256:(i + 1) * 256],
                                            start=(dc == 0), stop=False)
                        ins = te.matmul(sc[:, :], ones_f1[:, :],
                                        extra_row[:, i * 256:(i + 1) * 256],
                                        start=False, stop=True)
                        ins.then_inc(s_scp, 1)
                        te.wait_ge(s_exp, n_kt + 1)
                        for sub in range(2):
                            lt = pt_sb[:, n_kt % 2, sub * 128:(sub + 1) * 128]
                            ins = te.matmul(av_slice(sub)[:, 0:512], lt,
                                            kv_v[:, par, kk, 0:512],
                                            start=first, stop=last)
                            ins = te.matmul(av_slice(sub)[:, 512:1024], lt,
                                            kv_v[:, par, kk, 512:1024],
                                            start=first, stop=last)
                            ins = te.matmul(A(2048 + 512 * sub, 2049 + 512 * sub), lt,
                                            ones_bc[:, :], start=first, stop=last)
                        ins.then_inc(s_avp, 1)
                        n_kt += 1
                    g_seg += 1
                if i == 0:
                    te.wait_ge(s_ev, i + 1)
                    te.wait_ge(s_xe, 32)
                    te.wait_ge(s_pp_dve, PP_DVE_T0)
                    for c in range(JCN):
                        if c >= 2:
                            te.wait_ge(s_pp_dve, PP_DVE_T0 + c - 1)
                        Xc = X_sb[:, c // 4, (c % 4) * 128:(c % 4) * 128 + 128]
                        ins = te.matmul(A(2432, 2560) if c % 2 == 0 else A(2688, 2816),
                                        Xc, wexpT[:, c, :], start=True, stop=True)
                        ins.then_inc(s_pp_pe, 1)
                    te.wait_ge(s_pp_dve, PP_DVE_T0 + JCN)
                    for c in range(JCN):
                        Xc = X_sb[:, c // 4, (c % 4) * 128:(c % 4) * 128 + 128]
                        ins = te.matmul(A(2576 + c, 2577 + c), Xc,
                                        bexp_fm[:, c:c + 1], start=True, stop=True)
                    ins.then_inc(s_pp_pe, 1)
                if i == 2:
                    te.wait_ge(s_ev, i + 1)
                    te.wait_ge(s_pp_pool, 1)
                    te.wait_ge(s_xe, 32)
                    te.wait_ge(s_pp_dve, PP_DVE_T0 + JCN + 1)
                    for c in range(JCN):
                        if c >= 2:
                            te.wait_ge(s_pp_dve, PP_DVE_T0 + JCN + 1 + c - 1)
                        Ec = E_sb[:, c // 4, (c % 4) * 128:(c % 4) * 128 + 128]
                        ins = te.matmul(A(2432, 2560) if c % 2 == 0 else A(2688, 2816),
                                        Ec, ZD_FINAL[:, c, :], start=True, stop=True)
                        ins.then_inc(s_pp_pe, 1)
                if i == 4:
                    te.wait_ge(s_ev, i + 1)
                    te.wait_ge(s_pp_pool, 2)
                    for c in range(JCN):
                        if c >= 2:
                            te.wait_ge(s_pp_dve, PP_DVE_T0 + 2 * JCN + 1 + c - 1)
                        tb = tb0 if c % 2 == 0 else tb1
                        ins = te.transpose(tb[:, 0:128],
                                           Z2_FINAL[:, c, :], ident_bf[:, :])
                        ins.then_inc(s_pp_pe, 1)

        @sect(4, block.vector)
        def _(v):
            v.wait_ge(s_ld, LD_EARLY)
            n_kt = 0
            n_msk = 0
            for i in range(NBLK):
                for sg in range(4 * (i + 1)):
                    for kk in range(2):
                        ktg = sg * 2 + kk
                        if ktg >= 8 * i:
                            sc = A(3072, 3328) if n_kt % 2 == 0 else A(3584, 3840)
                            v.wait_ge(s_scp, n_kt + 1)
                            v.tensor_scalar(out=pen_sb[:, :], in0=qpos_bc[:, i, :],
                                            scalar1=kpos_f[:, ktg:ktg + 1],
                                            scalar2=PEN, op0=ALU.is_lt, op1=ALU.mult)
                            v.drain()
                            ins = v.tensor_add(sc[:, :], sc[:, :], pen_sb[:, :])
                            ins.then_inc(s_msk, 1)
                            v.drain()
                            n_msk += 1
                        n_kt += 1
                v.wait_ge(s_avp, n_kt)
                v.reciprocal(rz_sb[:, :], bass.AP(arena, 2048, [[4096, 128], [512, 2]]))
                v.drain()
                v.engine_nop().then_inc(s_rz, 1)
                if i == 0:
                    for c in range(JCN):
                        v.wait_ge(s_pp_pe, PP_PE_T0 + c + 1)
                        ps = A(2432, 2560) if c % 2 == 0 else A(2688, 2816)
                        ins = v.tensor_copy(z_a[:, c, :], ps[:, :])
                        ins.then_inc(s_pp_dve, 1)
                    v.wait_ge(s_pp_pe, PP_PE_T0 + JCN + 1)
                    ins = v.tensor_copy(b1_fm[:, :], A(2576, 2608))
                    ins.then_inc(s_pp_dve, 1)
                if i == 2:
                    for c in range(JCN):
                        v.wait_ge(s_pp_pe, PP_PE_T0 + JCN + 1 + c + 1)
                        ps = A(2432, 2560) if c % 2 == 0 else A(2688, 2816)
                        ins = v.tensor_copy(wdnc[:, c, :], ps[:, :])
                        ins.then_inc(s_pp_dve, 1)
                if i == 4:
                    for c in range(JCN):
                        v.wait_ge(s_pp_pe, PP_PE_T0 + 2 * JCN + 1 + c + 1)
                        tb = tb0 if c % 2 == 0 else tb1
                        ins = v.tensor_copy(wupc[:, c * 128:(c + 1) * 128],
                                            tb[:, 0:128])
                        ins.then_inc(s_pp_dve, 1)

        @sect(4, block.scalar)
        def _(s):
            s.wait_ge(s_pool, 1)
            n_kt = 0
            n_msk = 0
            for i in range(NBLK):
                for sg in range(4 * (i + 1)):
                    for kk in range(2):
                        ktg = sg * 2 + kk
                        sc = A(3072, 3328) if n_kt % 2 == 0 else A(3584, 3840)
                        s.wait_ge(s_scp, n_kt + 1)
                        if ktg >= 8 * i:
                            n_msk += 1
                            s.wait_ge(s_msk, n_msk)
                        if n_kt >= 2:
                            s.wait_ge(s_avp, n_kt - 1)
                        ins = s.activation(out=pt_sb[:, n_kt % 2, :], in_=sc[:, :],
                                           func=AF.Exp, bias=alibi_k[:, ktg:ktg + 1],
                                           scale=SCALE)
                        ins.then_inc(s_exp, 1)
                        n_kt += 1
                s.wait_ge(s_rz, i + 1)
                for sub in range(2):
                    ins = s.activation(out=attn_sb[:, 2 * i + sub, :],
                                       in_=av_slice(sub), func=AF.Identity,
                                       bias=0.0, scale=rz_sb[:, sub:sub + 1])
                ins.then_inc(s_ev, 1)

        @sect(4, block.gpsimd)
        def _(g):
            g.wait_ge(s_pp_dve, PP_DVE_T0 + JCN + 1)
            for k, s_idx in enumerate([7, 8, 9, 10, 11]):
                hi_stage(g, s_idx, up_seq[k], up_seq[k + 1], upv_fm, transposed=False)
            bb = [b1_fm, b2_fm]
            bcur = 0
            for s_idx in [7, 8, 9, 10, 11]:
                sc_ = 1 << (s_idx - 7)

                def col(t, c):
                    base = ((s_idx - 7) * 2 + t) * JCN + c
                    return upv_fm[:, base:base + 1]
                pairs = [(c, c ^ sc_) for c in range(JCN) if not (c & sc_)]
                for ca, cb in pairs:
                    g.tensor_mul(bt_f[:, ca:ca + 1], bb[bcur][:, cb:cb + 1], col(1, ca))
                    g.tensor_mul(bt_f[:, cb:cb + 1], bb[bcur][:, ca:ca + 1], col(1, cb))
                g.drain()
                for ca, cb in pairs:
                    g.tensor_mul(bb[1 - bcur][:, ca:ca + 1], bb[bcur][:, ca:ca + 1],
                                 col(0, ca))
                    g.tensor_mul(bb[1 - bcur][:, cb:cb + 1], bb[bcur][:, cb:cb + 1],
                                 col(0, cb))
                g.drain()
                for ca, cb in pairs:
                    g.tensor_add(bb[1 - bcur][:, ca:ca + 1], bb[1 - bcur][:, ca:ca + 1],
                                 bt_f[:, ca:ca + 1])
                    g.tensor_add(bb[1 - bcur][:, cb:cb + 1], bb[1 - bcur][:, cb:cb + 1],
                                 bt_f[:, cb:cb + 1])
                g.drain()
                bcur = 1 - bcur
            g.wait_ge(s_pp_dve, PP_DVE_T0)
            g.tensor_add(bmid_fm[:, :], bb[bcur][:, :], upb_fm[:, :])
            g.drain()
            g.engine_nop().then_inc(s_pp_pool, 1)
            g.engine_nop().then_inc(s_pool, 1)
            g.wait_ge(s_pp_dve, PP_DVE_T0 + JCN + 1 + JCN)
            g.dma_start(out=ccin_wdn.rearrange("(c p) d -> p c d", p=128),
                        in_=wdnc[:, :, :]).then_inc(s_ld, 16)
            n_ld[0] += 16
            g.wait_ge(s_ld, n_ld[0])
            g.collective_compute("AllGather", ALU.bypass, replica_groups=g8,
                                 ins=[ccin_wdn[:, :].opt()],
                                 outs=[wdn_all[:, :].opt()]).then_inc(s_cc, 1)
            g.wait_ge(s_pp_dve, PP_DVE_T0 + 2 * JCN + 1 + JCN)
            g.dma_start(out=ccin_wup[:, :], in_=wupc[:, :]).then_inc(s_ld, 16)
            n_ld[0] += 16
            g.wait_ge(s_ld, n_ld[0])
            g.collective_compute("AllGather", ALU.bypass, replica_groups=g8,
                                 ins=[ccin_wup[:, :].opt()],
                                 outs=[wup_all[:, :].opt()]).then_inc(s_cc, 1)

        if DEBUG:
            @block.gpsimd
            def _(g):
                def dump(o, i_):
                    g.dma_start(out=o, in_=i_).then_inc(s_ld, 16)
                    n_ld[0] += 16
                g.wait_ge(s_dve, DVE_XNF)
                dump(dbg["xnfm"][:, :], xn_fm.rearrange("p dc t -> p (dc t)"))
                g.wait_ge(s_act, ACT_QP)
                dump(dbg["qfm"][:, :], q_fm.rearrange("p dc t -> p (dc t)"))
                g.wait_ge(s_dve, DVE_EXT)
                dump(dbg["extra"][:, :], extra_row[:, :])
                g.wait_ge(s_xe, 32)
                dump(dbg["xsb"][:, :], X_sb.rearrange("p r c -> p (r c)"))
                dump(dbg["esb"][:, :], E_sb.rearrange("p r c -> p (r c)"))
                g.wait_ge(s_pp_dve, PP_DVE_T0 + JCN + 1 + JCN)
                dump(dbg["wdnc"][:, :], wdnc.rearrange("p c d -> p (c d)"))
                g.wait_ge(s_pp_dve, PP_DVE_T0 + 2 * JCN + 1 + JCN)
                dump(dbg["wupc"][:, :], wupc[:, :])
                g.wait_ge(s_pool, 3)
                dump(dbg["bmid"][:, :], bmid_fm[:, :])
                g.wait_ge(s_ev, NBLK)
                dump(dbg["attn"][:, :], attn_sb.rearrange("p t e -> p (t e)"))

        if not DEBUG:
            sc_att.close()
            sc_pp.close()
            sc_qg.close()
        if PH >= 6:
            hg = e(nc.sbuf_tensor("hg", [128, 16, 1024], BF16))
            bcon_row = e(nc.sbuf_tensor("bcon_row", [1, D], F32))
            dnb_row = e(nc.sbuf_tensor("dnb_row", [1, D], F32))
            y1 = e(nc.sbuf_tensor("y1", [128, 8, D], BF16))
            wup_buf = e(nc.sbuf_tensor("wup_buf", [128, 2, DC, 128], BF16))
            wcon_buf = e(nc.sbuf_tensor("wcon_buf", [128, 16, D], BF16))
            x_re = e(nc.sbuf_tensor("x_re", [128, 2, D], F32))
            outb = e(nc.sbuf_tensor("outb", [128, 2, D], F32))
            t1_sb = e(nc.sbuf_tensor("t1_sb", [128, 2, D], F32))

        if PH < 6:
            @block.gpsimd
            def _(g):
                if PH >= 1:
                    g.wait_ge(s_cc, 1)
                    g.wait_ge(s_dve, DVE_XNF)
                if PH >= 2:
                    g.wait_ge(s_cc, 2)
                    g.wait_ge(s_dve, DVE_EXT)
                    g.wait_ge(s_act, ACT_BND)
                if PH >= 3:
                    if SUB >= 1:
                        g.wait_ge(s_pp_dve, PP_DVE_T0)
                    if SUB >= 2:
                        g.wait_ge(s_xe, 32)
                    if SUB >= 3:
                        g.wait_ge(s_pp_pool, 1)
                if PH >= 4:
                    g.wait_ge(s_ev, NBLK)
                if PH >= 5:
                    g.wait_ge(s_cc, 5)
                    g.wait_ge(s_pool, 3)
                for t in range(NT):
                    g.dma_start(out=out_own[t * 128:(t + 1) * 128, :],
                                in_=attn_sb[:, t, :]).then_inc(s_st, 16)
                g.wait_ge(s_st, 16 * 16)

        @sect(6, block.sync)
        def _(sy):
            sy.wait_ge(s_cc, 5)
            sy.wait_ge(s_ev, NBLK)
            sy.dma_start(out=bcon_row[:, :], in_=b_con_in[None, :]).then_inc(s_wc, 16)
            sy.dma_start(out=dnb_row[:, :], in_=dn_bias_in[None, :]).then_inc(s_wc, 16)
            nxr = 0
            for th in range(2):
                for hf in range(2):
                    u = th * 2 + hf
                    for jtl in range(16):
                        jt = hf * 16 + jtl
                        gjt = u * 16 + jtl
                        if gjt >= 2:
                            sy.wait_ge(s_hps, gjt - 1)
                        sy.dma_start(
                            out=wup_buf[:, gjt % 2, :, :],
                            in_=wup_all.rearrange("(dc p) (jt c) -> jt p dc c",
                                                  p=128, c=128)[jt]).then_inc(s_wsP[gjt % 2], 16)
                    if u >= 1:
                        sy.wait_ge(s_yps, u * 8)
                    for jcl in range(16):
                        jc = hf * 16 + jcl
                        sy.dma_start(
                            out=wcon_buf[:, jcl, :].rearrange("p (r d) -> p r d", d=128),
                            in_=wdn_all.rearrange("(r c p) d -> c p r d",
                                                  r=8, p=128)[jc]).then_inc(s_wc, 16)
                    if hf == 1:
                        for tl in range(10):
                            if tl < 8:
                                tt = th * 8 + tl
                                if nxr >= 2:
                                    sy.wait_ge(s_asm, nxr - 1)
                                sy.dma_start(out=x_re[:, tt % 2, :],
                                             in_=x_own[tt * 128:(tt + 1) * 128, :]
                                             ).then_inc(s_xP[tt % 2], 16)
                                nxr += 1
                            if tl >= 2:
                                ts_ = th * 8 + tl - 2
                                sy.wait_ge(s_asm, ts_ + 1)
                                sy.dma_start(out=out_own[ts_ * 128:(ts_ + 1) * 128, :],
                                             in_=outb[:, ts_ % 2, :]).then_inc(s_stP[ts_ % 2], 16)
            sy.wait_ge(s_stA, 16 * 8)
            sy.wait_ge(s_stB, 16 * 8)

        @sect(6, block.tensor)
        def _(te):
            te.wait_ge(s_ev, NBLK)
            for th in range(2):
                for hf in range(2):
                    u = th * 2 + hf
                    for jtl in range(16):
                        gjt = u * 16 + jtl
                        gw = u * 16 + jtl
                        te.wait_ge(s_wsP[gw % 2], 16 * (gw // 2 + 1))
                        if gjt >= 2:
                            te.wait_ge(s_gel, gjt - 1)
                        hp = A(0, 1024) if gjt % 2 == 0 else A(1024, 2048)
                        for dc in range(DC):
                            for hh in range(2):
                                ins = te.matmul(
                                    hp[:, hh * 512:(hh + 1) * 512],
                                    wup_buf[:, gjt % 2, dc, :],
                                    xn_fm[:, dc, th * 1024 + hh * 512:
                                          th * 1024 + (hh + 1) * 512],
                                    start=(dc == 0), stop=(dc == DC - 1))
                        ins.then_inc(s_hps, 1)
                    te.wait_ge(s_wc, 32 + 16 * (u + 1) * 16)
                    te.wait_ge(s_gel, u * 16 + 16)
                    for tl in range(8):
                        gt = u * 8 + tl
                        if gt >= 2:
                            te.wait_ge(s_ycon, gt - 1)
                        yp = A(2048, 3072) if gt % 2 == 0 else A(3072, 4096)
                        for jcl in range(16):
                            for hh in range(2):
                                ins = te.matmul(
                                    yp[:, hh * 512:(hh + 1) * 512],
                                    hg[:, jcl, tl * 128:(tl + 1) * 128],
                                    wcon_buf[:, jcl, hh * 512:(hh + 1) * 512],
                                    start=(jcl == 0),
                                    stop=(hf == 0 and jcl == 15))
                        if hf == 1:
                            for hh in range(2):
                                ins = te.matmul(yp[:, hh * 512:(hh + 1) * 512],
                                                ones_f1[:, :],
                                                bcon_row[:, hh * 512:(hh + 1) * 512],
                                                start=False, stop=False)
                                ins = te.matmul(yp[:, hh * 512:(hh + 1) * 512],
                                                ones_f1[:, :],
                                                dnb_row[:, hh * 512:(hh + 1) * 512],
                                                start=False, stop=True)
                        ins.then_inc(s_yps, 1)

        @sect(6, block.scalar)
        def _(s):
            s.wait_ge(s_pool, 3)
            for th in range(2):
                for hf in range(2):
                    u = th * 2 + hf
                    if u >= 1:
                        s.wait_ge(s_yps, u * 8)
                    for jtl in range(16):
                        jt = hf * 16 + jtl
                        gjt = u * 16 + jtl
                        s.wait_ge(s_hps, gjt + 1)
                        hp = A(0, 1024) if gjt % 2 == 0 else A(1024, 2048)
                        ins = s.activation(out=hg[:, jtl, :], in_=hp[:, :],
                                           func=AF.Gelu, bias=bmid_fm[:, jt:jt + 1],
                                           scale=1.0)
                        ins.then_inc(s_gel, 1)
                    if hf == 0:
                        for tl in range(8):
                            gt = u * 8 + tl
                            s.wait_ge(s_yps, gt + 1)
                            yp = A(2048, 3072) if gt % 2 == 0 else A(3072, 4096)
                            ins = s.copy(y1[:, tl, :], yp[:, :])
                            ins.then_inc(s_ycon, 1)

        @sect(6, block.vector)
        def _(v):
            for th in range(2):
                u = th * 2 + 1
                for tl in range(8):
                    tt = th * 8 + tl
                    gt = u * 8 + tl
                    v.wait_ge(s_yps, gt + 1)
                    v.wait_ge(s_xP[tt % 2], 16 * (16 + tt // 2 + 1))
                    if tt >= 2:
                        v.wait_ge(s_stP[tt % 2], 16 * (tt // 2))
                    yp = A(2048, 3072) if gt % 2 == 0 else A(3072, 4096)
                    ins = v.scalar_tensor_tensor(out=t1_sb[:, tt % 2, :], in0=yp[:, :],
                                                 scalar=1.0, in1=y1[:, tl, :],
                                                 op0=ALU.mult, op1=ALU.add)
                    ins.then_inc(s_ycon, 1)
                    v.drain()
                    v.tensor_add(t1_sb[:, tt % 2, :], t1_sb[:, tt % 2, :],
                                 attn_sb[:, tt, :])
                    v.drain()
                    ins = v.tensor_add(outb[:, tt % 2, :], t1_sb[:, tt % 2, :],
                                       x_re[:, tt % 2, :])
                    ins.then_inc(s_asm, 1)
                    v.drain()
        if DEBUG:
            sc_att.close()
            sc_pp.close()
            sc_qg.close()
    return nc


_NC_CACHE = {}


def _get_nc():
    if "nc" not in _NC_CACHE:
        _NC_CACHE["nc"] = build_program()
    return _NC_CACHE["nc"]


def _stage_vectors(weights, dim):
    ns = weights.shape[0]
    out = np.zeros((ns, 2, dim), np.float32)
    j = np.arange(dim)
    for s in range(ns):
        sig = 1 << s
        w = np.asarray(weights[s], np.float32)
        pair = (j >> (s + 1)) * sig + (j & (sig - 1))
        bit = (j & sig) != 0
        out[s, 0] = np.where(bit, w[pair, 1, 1], w[pair, 0, 0])
        out[s, 1] = np.where(bit, w[pair, 1, 0], w[pair, 0, 1])
    return out


def kernel(L0_attn_x, L0_attn_W_qk, L0_attn_W_v, L0_attn_alibi_slope,
           L0_ffn_W_exp, L0_ffn_b_exp, L0_ffn_up_weights, L0_ffn_up_bias,
           L0_ffn_down_weights, L0_ffn_W_con, L0_ffn_b_con, L0_ffn_down_bias):
    x = np.ascontiguousarray(np.asarray(L0_attn_x, np.float32))
    upv = _stage_vectors(np.asarray(L0_ffn_up_weights), DH)
    dnv = _stage_vectors(np.asarray(L0_ffn_down_weights), DH)

    def fm_slices(vec):
        out = np.zeros((128, 5 * 2 * JCN), np.float32)
        for s in range(7, 12):
            for t in range(2):
                out[:, ((s - 7) * 2 + t) * JCN:((s - 7) * 2 + t + 1) * JCN] = \
                    vec[s, t].reshape(JCN, 128).T
        return np.ascontiguousarray(out)

    upvec_fm = fm_slices(upv)
    dnvec_fm = fm_slices(dnv)
    wqk_c = np.ascontiguousarray(np.asarray(L0_attn_W_qk, np.float32))
    wv_c = np.ascontiguousarray(np.asarray(L0_attn_W_v, np.float32))
    al_c = np.ascontiguousarray(np.asarray(L0_attn_alibi_slope, np.float32))
    wexp_c = np.asarray(L0_ffn_W_exp, np.float32)
    wcon_c = np.asarray(L0_ffn_W_con, np.float32)
    bexp_c = np.ascontiguousarray(np.asarray(L0_ffn_b_exp, np.float32))
    upb_c = np.ascontiguousarray(np.asarray(L0_ffn_up_bias, np.float32))
    bcon_c = np.ascontiguousarray(np.asarray(L0_ffn_b_con, np.float32))
    dnb_c = np.ascontiguousarray(np.asarray(L0_ffn_down_bias, np.float32))

    in_maps = []
    for core in range(8):
        b, j = core // 4, core % 4
        rows = np.concatenate([np.arange(POS(j, i), POS(j, i) + 256)
                               for i in range(NBLK)])
        in_maps.append({
            "x_own": np.ascontiguousarray(x[b][rows]),
            "wqk": wqk_c, "wv": wv_c, "alibi": al_c,
            "wexp_own": np.ascontiguousarray(wexp_c[j * 128:(j + 1) * 128, :]),
            "wcon_own": np.ascontiguousarray(wcon_c[:, j * 128:(j + 1) * 128]),
            "b_exp": bexp_c, "up_bias": upb_c, "b_con": bcon_c, "dn_bias": dnb_c,
            "qpos": np.ascontiguousarray(rows.astype(np.float32)),
            "upvec_sl": np.ascontiguousarray(upv[0:7, :, j * 512:(j + 1) * 512]),
            "dnvec_sl": np.ascontiguousarray(dnv[0:7, :, j * 512:(j + 1) * 512]),
            "upvec_fm": upvec_fm, "dnvec_fm": dnvec_fm,
        })

    nc = _get_nc()
    res = run_bass_kernel_spmd(nc, in_maps, core_ids=list(range(8)))
    out = np.zeros((B, S, D), np.float32)
    for core in range(8):
        b, j = core // 4, core % 4
        oo = res.results[core]["out_own"]
        for i in range(NBLK):
            p = POS(j, i)
            out[b, p:p + 256] = oo[i * 256:(i + 1) * 256]
    return out
